# version 15
# baseline (speedup 1.0000x reference)
import numpy as np
import concourse.bass as bass
import concourse.mybir as mybir
from concourse.bacc import Bacc
from concourse.bass_utils import run_bass_kernel_spmd

F32 = mybir.dt.float32
BF16 = mybir.dt.bfloat16
FP8 = mybir.dt.float8e4
U8 = mybir.dt.uint8
AF = mybir.ActivationFunctionType
ALU = mybir.AluOpType
AX = mybir.AxisListType

B_LOC = 2
T = 2048
D = 1024
NT = T // 128
DC = D // 128
NIC = T // 512
NEG = -1.0e30
MASK_W = 1152


def valid_chunks(jt):
    j0 = jt * 128
    return [ic for ic in range(NIC) if (ic + 1) * 512 - 1 >= j0]


E_OFF = {}
_off = 0
for _jt in range(NT):
    E_OFF[_jt] = _off
    _off += len(valid_chunks(_jt)) * 512
E_TOTAL = _off


def build_program():
    import ml_dtypes
    nc = Bacc("TRN2", target_bir_lowering=False)

    x_h = nc.dram_tensor("x", [B_LOC, T, D], F32, kind="ExternalInput")
    xf_h = nc.dram_tensor("xf", [16, T, D], F32, kind="ExternalInput")
    wq_h = nc.dram_tensor("wq", [D, D], F32, kind="ExternalInput")
    wk_h = nc.dram_tensor("wk", [D, D], F32, kind="ExternalInput")
    wv_h = nc.dram_tensor("wv", [D, D], F32, kind="ExternalInput")
    wf_h = nc.dram_tensor("wf", [D, D], F32, kind="ExternalInput")
    bq_h = nc.dram_tensor("bq", [1, D], F32, kind="ExternalInput")
    bk_h = nc.dram_tensor("bk", [1, D], F32, kind="ExternalInput")
    bv_h = nc.dram_tensor("bv", [1, D], F32, kind="ExternalInput")
    bf_h = nc.dram_tensor("bf", [1, D], F32, kind="ExternalInput")
    y_h = nc.dram_tensor("y", [B_LOC, T, D], F32, kind="ExternalOutput")

    wqkv_bf = nc.dram_tensor("wqkv_bf", [3, D, D], BF16, kind="Internal")
    w8T_d = nc.dram_tensor("w8T_d", [D, D], BF16, kind="Internal")
    acc_d = nc.dram_tensor("acc_d", [B_LOC, T, D], F32, kind="Internal")
    bounce = nc.dram_tensor("bounce", [1, 16], F32, kind="Internal")

    ident_h = nc.inline_tensor(np.eye(128, dtype=ml_dtypes.bfloat16), name="ident")
    uu = np.arange(MASK_W)[None, :]
    pp = np.arange(128)[:, None]
    band_np = np.where(uu - pp >= 512, 0.0, NEG).astype(ml_dtypes.bfloat16)
    band_h = nc.inline_tensor(band_np, name="band")
    ones_h = nc.inline_tensor(np.ones((1, 128), dtype=ml_dtypes.bfloat16), name="ones")

    x3 = x_h.ap()
    xf3 = xf_h.ap()
    y3 = y_h.ap()
    acc3 = acc_d.ap()

    from contextlib import ExitStack
    with ExitStack() as _ctx:
        E = _ctx.enter_context
        ident_sb = E(nc.sbuf_tensor("ident_sb", [128, 128], BF16))
        band_sb = E(nc.sbuf_tensor("band_sb", [128, MASK_W], BF16))
        ones_sb = E(nc.sbuf_tensor("ones_sb", [1, 128], BF16))
        bq_sb = E(nc.sbuf_tensor("bq_sb", [128, DC], F32))
        bk_sb = E(nc.sbuf_tensor("bk_sb", [128, DC], F32))
        bv_row = E(nc.sbuf_tensor("bv_row", [1, D], BF16))
        bf_row = E(nc.sbuf_tensor("bf_row", [1, D], BF16))
        mneg_st = E(nc.sbuf_tensor("mneg_st", [128, 2 * NT], F32))
        r_st = E(nc.sbuf_tensor("r_st", [128, 2 * NT], F32))
        s1p_st = E(nc.sbuf_tensor("s1p_st", [128, 2 * NT], F32))
        s2p_st = E(nc.sbuf_tensor("s2p_st", [128, 2 * NT], F32))
        mx_st = E(nc.sbuf_tensor("mx_st", [128, 2 * NT], F32))
        mn_st = E(nc.sbuf_tensor("mn_st", [128, 2 * NT], F32))
        tmp_c = E(nc.sbuf_tensor("tmp_c", [128, 2 * NT], F32))
        amax_acc = E(nc.sbuf_tensor("amax_acc", [128, 1], F32))
        amax_bc = E(nc.sbuf_tensor("amax_bc", [128, 1], F32))
        fmax = E(nc.sbuf_tensor("fmax", [128, 256], F32))
        fmin = E(nc.sbuf_tensor("fmin", [128, 256], F32))
        fsum = E(nc.sbuf_tensor("fsum", [128, 256], F32))
        fsq = E(nc.sbuf_tensor("fsq", [128, 256], F32))
        tmp_f = E(nc.sbuf_tensor("tmp_f", [128, 256], F32))
        tmp_g = E(nc.sbuf_tensor("tmp_g", [128, 256], F32))
        sx_sb = E(nc.sbuf_tensor("sx_sb", [128, 1], F32))
        sxh_sb = E(nc.sbuf_tensor("sxh_sb", [128, 1], F32))
        thr_sb = E(nc.sbuf_tensor("thr_sb", [128, 1], F32))
        row_sc = E(nc.sbuf_tensor("row_sc", [1, 128], F32))
        swq_sb = E(nc.sbuf_tensor("swq_sb", [128, 1], F32))
        swqh_sb = E(nc.sbuf_tensor("swqh_sb", [128, 1], F32))
        thrw_sb = E(nc.sbuf_tensor("thrw_sb", [128, 1], F32))
        t1_sb = E(nc.sbuf_tensor("t1_sb", [128, 1], F32))
        t2_sb = E(nc.sbuf_tensor("t2_sb", [128, 1], F32))
        cpx_sb = E(nc.sbuf_tensor("cpx_sb", [128, 1], F32))
        cnx_sb = E(nc.sbuf_tensor("cnx_sb", [128, 1], F32))
        cpw_sb = E(nc.sbuf_tensor("cpw_sb", [128, 1], F32))
        cnw_sb = E(nc.sbuf_tensor("cnw_sb", [128, 1], F32))
        sprod_sb = E(nc.sbuf_tensor("sprod_sb", [128, 1], F32))
        scinv_sb = E(nc.sbuf_tensor("scinv_sb", [128, 1], F32))
        bf_rs = E(nc.sbuf_tensor("bf_rs", [1, D], BF16))
        xw = E(nc.sbuf_tensor("xw", [128, 2, D], F32))
        xn_f = E(nc.sbuf_tensor("xn_f", [128, D], F32))
        accw = E(nc.sbuf_tensor("accw", [128, 2, D], F32))
        msk8 = E(nc.sbuf_tensor("msk8", [128, D], U8))
        lo8_t = E(nc.sbuf_tensor("lo8", [128, D], FP8))
        hi8_t = E(nc.sbuf_tensor("hi8", [128, D], FP8))
        hi2_t = E(nc.sbuf_tensor("hi2", [128, D], BF16))
        xqT_st = E(nc.sbuf_tensor("xqT_st", [128, D], BF16))
        bscr = E(nc.sbuf_tensor("bscr", [128, T], BF16))
        ereg = E(nc.sbuf_tensor("ereg", [128, 24576], BF16))
        qt_r = E(nc.sbuf_tensor("qt_r", [128, DC, T], BF16))
        kt_r = E(nc.sbuf_tensor("kt_r", [128, DC, T], BF16))
        v_r = E(nc.sbuf_tensor("v_r", [128, NT, D], BF16))
        mrun = E(nc.sbuf_tensor("mrun", [128, 1], F32))
        mnegs = E(nc.sbuf_tensor("mnegs", [128, NIC], F32))
        msave = E(nc.sbuf_tensor("msave", [128, NIC], F32))
        ssave = E(nc.sbuf_tensor("ssave", [128, NIC], F32))
        gall = E(nc.sbuf_tensor("gall", [128, NT, NIC], F32))
        tinyA = E(nc.sbuf_tensor("tinyA", [128, NIC], F32))
        tinyB = E(nc.sbuf_tensor("tinyB", [128, NIC], F32))
        lsum = E(nc.sbuf_tensor("lsum", [128, 1], F32))
        ps_a = E(nc.psum_tensor("ps_a", [128, 512], F32))
        ps_b = E(nc.psum_tensor("ps_b", [128, 512], F32))
        ps_c = E(nc.psum_tensor("ps_c", [128, 512], F32))
        ps_d = E(nc.psum_tensor("ps_d", [128, 512], F32))
        ps_tr = E(nc.psum_tensor("ps_tr", [128, 1024], BF16))
        ps_tr2 = E(nc.psum_tensor("ps_tr2", [128, 1024], BF16))
        PSTR = (ps_tr, ps_tr2)
        sem_cst = E(nc.semaphore("sem_cst"))
        sem_x = [E(nc.semaphore("sem_x0")), E(nc.semaphore("sem_x1"))]
        sem_f1 = E(nc.semaphore("sem_f1"))
        sem_f2 = E(nc.semaphore("sem_f2"))
        sem_w = E(nc.semaphore("sem_w"))
        sem_wl = [E(nc.semaphore("sem_wl0")), E(nc.semaphore("sem_wl1"))]
        sem_wp = [E(nc.semaphore("sem_wp0")), E(nc.semaphore("sem_wp1"))]
        sem_qp = E(nc.semaphore("sem_qp"))
        sem_pk = [E(nc.semaphore("sem_pk0")), E(nc.semaphore("sem_pk1"))]
        sem_acc = [E(nc.semaphore("sem_acc0")), E(nc.semaphore("sem_acc1"))]
        sem_y = [E(nc.semaphore("sem_y0")), E(nc.semaphore("sem_y1"))]
        sem_sg = E(nc.semaphore("sem_sg"))
        sem_v = E(nc.semaphore("sem_v"))
        sem_s = E(nc.semaphore("sem_s"))
        sem_t = E(nc.semaphore("sem_t"))
        sem_pvv = E(nc.semaphore("sem_pvv"))
        sem_pvs = E(nc.semaphore("sem_pvs"))
        block = E(nc.Block())

        PS = {"a": ps_a, "b": ps_b, "c": ps_c, "d": ps_d}
        BANKS = ("a", "b", "c", "d")

        def sqw():
            return xn_f[:]

        def wld(i):
            return xw[:, i]

        def wcast(i):
            return bscr[:, i * D:(i + 1) * D]

        def xnT(dcc):
            return ereg[:, dcc * T:(dcc + 1) * T]

        def wbf(dcc):
            return ereg[:, DC * T + dcc * D: DC * T + (dcc + 1) * D]

        def e_chunk(jt, ic):
            fic = valid_chunks(jt)[0]
            o = E_OFF[jt] + (ic - fic) * 512
            return ereg[:, o:o + 512]

        def e_cols(jt, i0, w):
            ic = i0 // 512
            fic = valid_chunks(jt)[0]
            o = E_OFF[jt] + (ic - fic) * 512 + (i0 - ic * 512)
            return ereg[:, o:o + w]

        def q_bf():
            return bscr[:, 0:D]

        def xn_bf():
            return bscr[:, D:2 * D]

        def pstage(jt):
            return bscr[:, jt * 128:(jt + 1) * 128]

        def qt(kc):
            return qt_r[:, kc]

        def kt(kc):
            return kt_r[:, kc]

        def vt(jt):
            return v_r[:, jt]

        class St:
            cst = 0
            x = [0, 0]
            f1 = 0
            f2 = 0
            w = 0
            wl = [0, 0]
            wp = [0, 0]
            qp = 0
            pk = [0, 0]
            acc = [0, 0]
            y = [0, 0]
            sg = 0
            v = 0
            s = 0
            t = 0
            pv_v = 0
            pv_s = 0
            bank_evac = {"a": None, "b": None, "c": None, "d": None}
            pstr_evac = [0, 0]

        prog = {k: [] for k in ("sync", "gpsimd", "vector", "scalar", "tensor")}

        _ENG_SEM = {"vector": (sem_v, "v"), "scalar": (sem_s, "s"), "tensor": (sem_t, "t")}

        def emit(engine, fn):
            if engine in _ENG_SEM:
                sem, attr = _ENG_SEM[engine]
                cnt = getattr(St, attr)

                def wrapped(eng, fn=fn, sem=sem, cnt=cnt):
                    if cnt:
                        eng.wait_ge(sem, cnt)
                    fn(eng)
                prog[engine].append(wrapped)
            else:
                prog[engine].append(fn)

        def c_loads(sync):
            sync.dma_start(out=ident_sb[:], in_=ident_h.ap()).then_inc(sem_cst, 16)
            sync.dma_start(out=band_sb[:], in_=band_h.ap()).then_inc(sem_cst, 16)
            sync.dma_start(out=ones_sb[:], in_=ones_h.ap()).then_inc(sem_cst, 16)
            for kc in range(DC):
                sync.dma_start(out=bq_sb[:, kc:kc + 1],
                               in_=bq_h.ap()[0:1, kc * 128:(kc + 1) * 128].rearrange("a b -> (a b)").unsqueeze(1)).then_inc(sem_cst, 16)
                sync.dma_start(out=bk_sb[:, kc:kc + 1],
                               in_=bk_h.ap()[0:1, kc * 128:(kc + 1) * 128].rearrange("a b -> (a b)").unsqueeze(1)).then_inc(sem_cst, 16)
            sync.dma_start(out=xn_f[0:1, :], in_=bv_h.ap()).then_inc(sem_cst, 16)
        emit("sync", c_loads)
        St.cst += (4 + 2 * DC) * 16
        cst1 = St.cst

        def c_cast1(v, need=cst1):
            v.wait_ge(sem_cst, need)
            v.tensor_copy(bv_row[:], xn_f[0:1, :]).then_inc(sem_v, 1)
        emit("vector", c_cast1)
        St.v += 1
        c_cast1_v = St.v

        def c_load2(sync, need=c_cast1_v):
            sync.wait_ge(sem_v, need)
            sync.dma_start(out=xn_f[0:1, :], in_=bf_h.ap()).then_inc(sem_cst, 16)
        emit("sync", c_load2)
        St.cst += 16
        cst2 = St.cst

        def c_cast2(v, need=cst2):
            v.wait_ge(sem_cst, need)
            v.tensor_copy(bf_row[:], xn_f[0:1, :]).then_inc(sem_v, 1)
        emit("vector", c_cast2)
        St.v += 1
        c_cast2_v = St.v

        cast_done = {}
        for wi, wsrc in enumerate((wq_h, wk_h, wv_h)):
            for tc in range(DC):
                k = wi * DC + tc
                sl = slice(tc * 128, (tc + 1) * 128)
                p = k % 2

                def ld(sync, wsrc=wsrc, sl=sl, k=k, p=p):
                    if k >= 2:
                        sync.wait_ge(sem_v, cast_done[k - 2])
                    else:
                        sync.wait_ge(sem_v, c_cast1_v)
                    sync.dma_start(out=wld(p), in_=wsrc.ap()[sl, :]).then_inc(sem_wl[p], 16)
                emit("sync", ld)
                St.wl[p] += 16
                need_wl = St.wl[p]

                def cast(v, k=k, p=p, need_wl=need_wl, need_wp=St.wp[p]):
                    v.wait_ge(sem_wl[p], need_wl)
                    if need_wp:
                        v.wait_ge(sem_wp[p], need_wp)
                    v.tensor_copy(wcast(p), wld(p)).then_inc(sem_v, 1)
                emit("vector", cast)
                St.v += 1
                cast_done[k] = St.v
                need_v = St.v

                def park(g, sl=sl, k=k, wi=wi, p=p, need_v=need_v):
                    g.wait_ge(sem_v, need_v)
                    g.dma_start(out=wqkv_bf.ap()[wi, sl, :], in_=wcast(p)).then_inc(sem_wp[p], 16)
                emit("gpsimd", park)
                St.wp[p] += 16
        wp_final = (St.wp[0], St.wp[1])

        wamax_done = {}
        for tc in range(DC):
            sl = slice(tc * 128, (tc + 1) * 128)
            p = tc % 2

            def ldf(sync, sl=sl, tc=tc, p=p):
                if tc >= 2:
                    sync.wait_ge(sem_v, wamax_done[tc - 2])
                else:
                    sync.wait_ge(sem_v, cast_done[22 + tc])
                sync.dma_start(out=wld(p), in_=wf_h.ap()[sl, :]).then_inc(sem_wl[p], 16)
            emit("sync", ldf)
            St.wl[p] += 16
            need_wl = St.wl[p]

            def red(v, tc=tc, p=p, need_wl=need_wl):
                v.wait_ge(sem_wl[p], need_wl)
                if tc == 0:
                    v.reduce_max(amax_acc[:], wld(0), axis=AX.X,
                                 apply_absolute_value=True).then_inc(sem_v, 1)
                else:
                    v.reduce_max(t1_sb[:], wld(p), axis=AX.X,
                                 apply_absolute_value=True).then_inc(sem_v, 1)
                    v.tensor_max(amax_acc[:], amax_acc[:], t1_sb[:]).then_inc(sem_v, 1)
            emit("vector", red)
            St.v += 1 if tc == 0 else 2
            wamax_done[tc] = St.v
        wamax_v = St.v

        def wam_x(g, need=wamax_v):
            g.wait_ge(sem_v, need)
            g.dma_start(out=row_sc[:], in_=amax_acc[:, 0:1]).then_inc(sem_sg, 16)
        emit("gpsimd", wam_x)
        St.sg += 16
        need_sg = St.sg

        def wam_r(v, need=need_sg):
            v.wait_ge(sem_sg, need)
            v.reduce_max(row_sc[0:1, 0:1], row_sc[:], axis=AX.X).then_inc(sem_v, 1)
        emit("vector", wam_r)
        St.v += 1
        need_v = St.v

        def wam_b(g, need=need_v):
            g.wait_ge(sem_v, need)
            g.dma_start(out=bounce.ap()[0:1, 0:1], in_=row_sc[0:1, 0:1]).then_inc(sem_sg, 16)
        emit("gpsimd", wam_b)
        St.sg += 16
        need_sg = St.sg

        def wam_bc(sync, need=need_sg, gate=c_cast2_v):
            sync.wait_ge(sem_v, gate)
            sync.wait_ge(sem_sg, need)
            sync.dma_start(out=swq_sb[:, 0:1],
                           in_=bounce.ap()[0:1, 0:1].partition_broadcast(128)).then_inc(sem_cst, 16)
        emit("sync", wam_bc)
        St.cst += 16
        cst3 = St.cst

        def wscale(v, need=cst3, base=St.v):
            v.wait_ge(sem_cst, need)
            v.tensor_scalar(t1_sb[:], swq_sb[:, 0:1], 1e-12, None, ALU.max).then_inc(sem_v, 1)
            v.tensor_tensor(thrw_sb[:], t1_sb[:], t1_sb[:], ALU.mult).then_inc(sem_v, 1)
            v.tensor_scalar(thrw_sb[:], thrw_sb[:], 4.0 / 49.0, None, ALU.mult).then_inc(sem_v, 1)
            v.wait_ge(sem_v, base + 1)
            v.reciprocal(t2_sb[:], t1_sb[:]).then_inc(sem_v, 1)
            v.tensor_scalar(swq_sb[:], t2_sb[:], 448.0, None, ALU.mult).then_inc(sem_v, 1)
            v.wait_ge(sem_v, base + 4)
            v.tensor_scalar(swqh_sb[:], t2_sb[:], 224.0, None, ALU.mult).then_inc(sem_v, 1)
            v.tensor_scalar(cpw_sb[:], t1_sb[:], 240.0 / 448.0 * 0.99999, None, ALU.mult).then_inc(sem_v, 1)
            v.tensor_scalar(cnw_sb[:], cpw_sb[:], -1.0, None, ALU.mult).then_inc(sem_v, 1)
        emit("vector", wscale)
        St.v += 8
        wscale_v = St.v

        wq_vdone = {}
        wq_tdone = {}
        wq_qp = {}
        wq_msk_v = {}
        for tc in range(DC):
            sl = slice(tc * 128, (tc + 1) * 128)
            p = tc % 2

            def ldf2(sync, sl=sl, tc=tc, p=p):
                if tc >= 2:
                    sync.wait_ge(sem_v, wq_vdone[tc - 2])
                else:
                    sync.wait_ge(sem_v, wamax_done[6 + tc])
                sync.dma_start(out=wld(p), in_=wf_h.ap()[sl, :]).then_inc(sem_wl[p], 16)
            emit("sync", ldf2)
            St.wl[p] += 16
            need_wl = St.wl[p]

            def wclip(v, tc=tc, p=p, need_wl=need_wl, need_v=wscale_v):
                v.wait_ge(sem_wl[p], need_wl)
                if tc == 0:
                    v.wait_ge(sem_v, need_v)
                v.tensor_scalar(accw[:, 0], wld(p), cpw_sb[:, 0:1], cnw_sb[:, 0:1],
                                ALU.min, ALU.max).then_inc(sem_v, 1)
            emit("vector", wclip)
            St.v += 1
            wclip_v = St.v

            def q_s(s, tc=tc, p=p, need_wl=need_wl, need_clip=wclip_v,
                    need_qv=(wq_vdone.get(tc - 1) if tc >= 1 else None)):
                s.wait_ge(sem_wl[p], need_wl)
                s.wait_ge(sem_v, need_clip)
                if need_qv is not None:
                    s.wait_ge(sem_v, need_qv)
                s.activation(lo8_t[:], accw[:, 0], AF.Copy, scale=swq_sb[:, 0:1]).then_inc(sem_s, 1)
                s.activation(hi8_t[:], wld(p), AF.Copy, scale=swqh_sb[:, 0:1]).then_inc(sem_s, 1)
            emit("scalar", q_s)
            St.s += 2
            need_s = St.s

            wq_msk_v[tc] = St.v + 2
            def q_v(v, tc=tc, p=p, need_s=need_s, wpf=wp_final):
                v.wait_ge(sem_s, need_s)
                if tc == 0:
                    v.wait_ge(sem_wp[0], wpf[0])
                    v.wait_ge(sem_wp[1], wpf[1])
                if tc >= 1:
                    v.wait_ge(sem_t, wq_tdone[tc - 1])
                v.scalar_tensor_tensor(hi2_t[:], wld(p), 1.0, wld(p),
                                       ALU.bypass, ALU.mult).then_inc(sem_v, 1)
                v.tensor_scalar(msk8[:], hi2_t[:], thrw_sb[:, 0:1], None,
                                ALU.is_gt).then_inc(sem_v, 1)
                v.tensor_scalar(hi2_t[:], hi8_t[:], 2.0, None, ALU.mult).then_inc(sem_v, 1)
                v.wait_ge(sem_v, wq_msk_v[tc])
                v.select(q_bf(), msk8[:], hi2_t[:], lo8_t[:], add_drain=True).then_inc(sem_v, 1)
            emit("vector", q_v)
            St.v += 4
            wq_vdone[tc] = St.v
            need_v = St.v

            def q_t(t, need_v=need_v, need_pstr=St.pstr_evac[0]):
                t.wait_ge(sem_v, need_v)
                if need_pstr:
                    t.wait_ge(sem_v, need_pstr)
                ins = None
                for blk in range(8):
                    ins = t.transpose(ps_tr[:, blk * 128:(blk + 1) * 128],
                                      q_bf()[:, blk * 128:(blk + 1) * 128], ident_sb[:])
                ins.then_inc(sem_t, 1)
            emit("tensor", q_t)
            St.t += 1
            wq_tdone[tc] = St.t
            need_t = St.t

            def q_e(v, tc=tc, need_t=need_t, need_qp=St.qp):
                v.wait_ge(sem_t, need_t)
                if need_qp:
                    v.wait_ge(sem_qp, need_qp)
                v.tensor_copy(xqT_st[:], ps_tr[:]).then_inc(sem_v, 1)
            emit("vector", q_e)
            St.v += 1
            St.pstr_evac[0] = St.v
            need_v = St.v

            def q_p(g, tc=tc, need_v=need_v):
                g.wait_ge(sem_v, need_v)
                for blk in range(8):
                    g.dma_start(out=w8T_d.ap()[blk * 128:(blk + 1) * 128,
                                               tc * 128:(tc + 1) * 128],
                                in_=xqT_st[:, blk * 128:(blk + 1) * 128]).then_inc(sem_qp, 16)
            emit("gpsimd", q_p)
            St.qp += 8 * 16
            wq_qp[tc] = St.qp
        qp_final = St.qp
        w2_end_t = St.t

        ph0_vdone = {}
        ph0_sdone = {}
        ph0_gate_s = St.s
        for gt in range(256):
            b, tt = divmod(gt, NT)
            rs = slice(tt * 128, (tt + 1) * 128)
            c = slice(gt, gt + 1)
            slot = gt % 4
            buf = xw[:, slot] if slot < 2 else accw[:, slot - 2]

            def ld0(sync, b=b, rs=rs, gt=gt, slot=slot, buf=buf):
                if gt >= 4:
                    sync.wait_ge(sem_v, ph0_vdone[gt - 4])
                    sync.wait_ge(sem_s, ph0_sdone[gt - 4])
                elif gt < 2:
                    sync.wait_ge(sem_v, wq_vdone[6 + gt])
                elif gt == 2:
                    sync.wait_ge(sem_s, ph0_gate_s)
                sync.dma_start(out=buf, in_=xf3[b, rs, :]).then_inc(
                    (sem_x[0], sem_x[1], sem_f1, sem_f2)[slot], 16)
            emit("sync", ld0)
            if slot == 0:
                St.x[0] += 16
                xsem, need_x = sem_x[0], St.x[0]
            elif slot == 1:
                St.x[1] += 16
                xsem, need_x = sem_x[1], St.x[1]
            elif slot == 2:
                St.f1 += 16
                xsem, need_x = sem_f1, St.f1
            else:
                St.f2 += 16
                xsem, need_x = sem_f2, St.f2

            vsum = (gt % 8 == 0)

            def sq0(s, gt=gt, c=c, buf=buf, xsem=xsem, need_x=need_x, vsum=vsum):
                s.wait_ge(xsem, need_x)
                if gt == 0:
                    s.wait_ge(sem_v, c_cast2_v)
                if not vsum:
                    s.activation(sqw(), buf, AF.Identity,
                                 accum_out=fsum[:, c]).then_inc(sem_s, 1)
                s.activation(sqw(), buf, AF.Square,
                             accum_out=fsq[:, c]).then_inc(sem_s, 1)
            emit("scalar", sq0)
            St.s += 1 if vsum else 2
            ph0_sdone[gt] = St.s

            def st0(v, gt=gt, c=c, buf=buf, xsem=xsem, need_x=need_x, vsum=vsum):
                v.wait_ge(xsem, need_x)
                v.reduce_max(fmax[:, c], buf, axis=AX.X).then_inc(sem_v, 1)
                v.tensor_reduce(fmin[:, c], buf, AX.X, ALU.min).then_inc(sem_v, 1)
                if vsum:
                    v.reduce_sum(fsum[:, c], buf, axis=AX.X).then_inc(sem_v, 1)
            emit("vector", st0)
            St.v += 3 if vsum else 2
            ph0_vdone[gt] = St.v
        ph0_all_s = St.s

        slA = slice(0, 256)

        def comb1(v, need_s=ph0_all_s, base=St.v):
            v.wait_ge(sem_s, need_s)
            k = [base]

            def w(ins):
                k[0] += 1
                ins.then_inc(sem_v, 1)
                v.wait_ge(sem_v, k[0])
            w(v.tensor_scalar(tmp_f[:, slA], fsum[:, slA], -1.0 / D, None, ALU.mult))
            w(v.tensor_scalar(fsq[:, slA], fsq[:, slA], 1.0 / D, None, ALU.mult))
            w(v.scalar_tensor_tensor(fsum[:, slA], tmp_f[:, slA], 1.0, tmp_f[:, slA],
                                     ALU.bypass, ALU.mult))
            w(v.tensor_sub(fsq[:, slA], fsq[:, slA], fsum[:, slA]))
            w(v.tensor_scalar(fsq[:, slA], fsq[:, slA], 1e-5, None, ALU.add))
            w(v.reciprocal(fsum[:, slA], fsq[:, slA]))
        emit("vector", comb1)
        St.v += 6
        need_v = St.v

        def comb_sqrt(s, need=need_v):
            s.wait_ge(sem_v, need)
            s.activation(fsum[:, slA], fsum[:, slA], AF.Sqrt).then_inc(sem_s, 1)
        emit("scalar", comb_sqrt)
        St.s += 1
        need_s = St.s

        def comb2(v, need=need_s, base=St.v):
            v.wait_ge(sem_s, need)
            k = [base]

            def w(ins):
                k[0] += 1
                ins.then_inc(sem_v, 1)
                v.wait_ge(sem_v, k[0])
            for _ in range(2):
                w(v.scalar_tensor_tensor(tmp_g[:, slA], fsum[:, slA], 1.0, fsum[:, slA],
                                         ALU.bypass, ALU.mult))
                w(v.tensor_tensor(tmp_g[:, slA], tmp_g[:, slA], fsq[:, slA], ALU.mult))
                w(v.tensor_scalar(tmp_g[:, slA], tmp_g[:, slA], -0.5, 1.5, ALU.mult, ALU.add))
                w(v.tensor_tensor(fsum[:, slA], fsum[:, slA], tmp_g[:, slA], ALU.mult))
            w(v.tensor_tensor(fmax[:, slA], fmax[:, slA], tmp_f[:, slA], ALU.add))
            w(v.tensor_tensor(fmin[:, slA], fmin[:, slA], tmp_f[:, slA], ALU.add))
            w(v.tensor_scalar(fmin[:, slA], fmin[:, slA], -1.0, None, ALU.mult))
            w(v.tensor_max(fmax[:, slA], fmax[:, slA], fmin[:, slA]))
            w(v.tensor_tensor(fmax[:, slA], fmax[:, slA], fsum[:, slA], ALU.mult))
            w(v.reduce_max(amax_acc[:], fmax[:, slA], axis=AX.X))
            w(v.tensor_scalar(tmp_g[:, slA], fsum[:, slA], 1.0, None, ALU.add))
            w(v.tensor_tensor(fmin[:, slA], tmp_f[:, slA], fsum[:, slA], ALU.mult))
        emit("vector", comb2)
        St.v += 16
        amax_v = St.v

        cid = nc.partition_id([mybir.EngineType.Pool])

        def own_cp(g, need=amax_v):
            g.wait_ge(sem_v, need)
            g.dma_start(out=mneg_st[:, 0:2 * NT], in_=tmp_f[:, bass.ds(cid * 32, 32)]).then_inc(sem_sg, 16)
            g.dma_start(out=r_st[:, 0:2 * NT], in_=fsum[:, bass.ds(cid * 32, 32)]).then_inc(sem_sg, 16)
            g.dma_start(out=s1p_st[:, 0:2 * NT], in_=tmp_g[:, bass.ds(cid * 32, 32)]).then_inc(sem_sg, 16)
            g.dma_start(out=s2p_st[:, 0:2 * NT], in_=fmin[:, bass.ds(cid * 32, 32)]).then_inc(sem_sg, 16)
        emit("gpsimd", own_cp)
        St.sg += 64

        def am_x(g, need=amax_v):
            g.wait_ge(sem_v, need)
            g.dma_start(out=row_sc[:], in_=amax_acc[:, 0:1]).then_inc(sem_sg, 16)
        emit("gpsimd", am_x)
        St.sg += 16
        own_sg = St.sg
        need_sg = St.sg

        def am_r(v, need=need_sg):
            v.wait_ge(sem_sg, need)
            v.reduce_max(row_sc[0:1, 0:1], row_sc[:], axis=AX.X).then_inc(sem_v, 1)
        emit("vector", am_r)
        St.v += 1
        need_v = St.v

        def am_b(g, need=need_v):
            g.wait_ge(sem_v, need)
            g.dma_start(out=bounce.ap()[0:1, 1:2], in_=row_sc[0:1, 0:1]).then_inc(sem_sg, 16)
        emit("gpsimd", am_b)
        St.sg += 16
        need_sg = St.sg

        def am_bc(sync, need=need_sg, gate=wscale_v):
            sync.wait_ge(sem_v, gate)
            sync.wait_ge(sem_sg, need)
            sync.dma_start(out=amax_bc[:, 0:1],
                           in_=bounce.ap()[0:1, 1:2].partition_broadcast(128)).then_inc(sem_cst, 16)
        emit("sync", am_bc)
        St.cst += 16
        cst4 = St.cst

        def gscale(v, need=cst4, own=own_sg, base=St.v):
            v.wait_ge(sem_cst, need)
            v.wait_ge(sem_sg, own)
            k = [base]

            def w(ins):
                k[0] += 1
                ins.then_inc(sem_v, 1)
                v.wait_ge(sem_v, k[0])
            w(v.tensor_scalar(t1_sb[:], amax_bc[:, 0:1], 1e-12, None, ALU.max))
            w(v.tensor_tensor(thr_sb[:], t1_sb[:], t1_sb[:], ALU.mult))
            w(v.tensor_scalar(thr_sb[:], thr_sb[:], 4.0 / 49.0, None, ALU.mult))
            w(v.reciprocal(t2_sb[:], t1_sb[:]))
            w(v.tensor_scalar(sx_sb[:], t2_sb[:], 448.0, None, ALU.mult))
            w(v.tensor_scalar(sxh_sb[:], t2_sb[:], 224.0, None, ALU.mult))
            w(v.tensor_scalar(cpx_sb[:], t1_sb[:], 240.0 / 448.0 * 0.99999, None, ALU.mult))
            w(v.tensor_scalar(cnx_sb[:], cpx_sb[:], -1.0, None, ALU.mult))
            w(v.tensor_tensor(sprod_sb[:], sx_sb[:], swq_sb[:], ALU.mult))
            w(v.reciprocal(scinv_sb[:], sprod_sb[:]))
            w(v.tensor_scalar(bf_rs[:], bf_row[:], sprod_sb[0:1, 0:1], None, ALU.mult))
        emit("vector", gscale)
        St.v += 11
        gscale_v = St.v

        last_wbf_reader_t = 0
        last_resc_v = 0
        last_pv_t = 0
        acc_parked = {}
        out_done = {}
        for b in range(B_LOC):
            for tc in range(DC):
                def ldw8(sync, tc=tc, b=b, need_qp=qp_final,
                         need_t=last_wbf_reader_t, need_v=last_resc_v):
                    if tc == 0:
                        if b == 0:
                            sync.wait_ge(sem_qp, need_qp)
                        if need_t:
                            sync.wait_ge(sem_t, need_t)
                        if need_v:
                            sync.wait_ge(sem_v, need_v)
                    sync.dma_start(out=wbf(tc),
                                   in_=w8T_d.ap()[tc * 128:(tc + 1) * 128, :]).then_inc(sem_w, 16)
                emit("sync", ldw8)
                St.w += 16
            w8T_in = St.w

            tr2_done = {}
            for tt in range(NT):
                gt = b * NT + tt
                rs = slice(tt * 128, (tt + 1) * 128)
                c = slice(gt, gt + 1)
                p = tt % 2

                def ld1(sync, b=b, rs=rs, tt=tt, p=p,
                        gate_pk=(acc_parked.get((b, tt - 2)) if tt >= 2 else
                                 acc_parked.get((b - 1, NT - 2 + tt))),
                        gate_v=(ph0_vdone[252 + tt] if (b == 0 and tt < 2) else None),
                        gate_s=(ph0_sdone[252 + tt] if (b == 0 and tt < 2) else None)):
                    if gate_pk is not None:
                        sync.wait_ge(sem_pk[p], gate_pk)
                    if gate_v is not None:
                        sync.wait_ge(sem_v, gate_v)
                    if gate_s is not None:
                        sync.wait_ge(sem_s, gate_s)
                    sync.dma_start(out=xw[:, p], in_=x3[b, rs, :]).then_inc(sem_x[p], 16)
                emit("sync", ld1)
                St.x[p] += 16
                need_x = St.x[p]

                def xnS(s, c=c, p=p, need_x=need_x,
                        first=(b == 0 and tt == 0), gv=gscale_v):
                    s.wait_ge(sem_x[p], need_x)
                    if first:
                        s.wait_ge(sem_v, gv)
                    s.activation(xn_f[:], xw[:, p], AF.Identity,
                                 scale=r_st[:, c], bias=s2p_st[:, c]).then_inc(sem_s, 1)
                emit("scalar", xnS)
                St.s += 1
                need_xs = St.s

                def xn1(v, p=p, need_xs=need_xs,
                        need_pk=(acc_parked.get((b, tt - 2))),
                        need_y=(out_done.get((b - 1, NT - 2 + tt))
                                if (b >= 1 and tt < 2) else None)):
                    v.wait_ge(sem_s, need_xs)
                    if need_pk is not None:
                        v.wait_ge(sem_pk[p], need_pk)
                    if need_y is not None:
                        v.wait_ge(sem_y[p], need_y)
                    v.scalar_tensor_tensor(hi2_t[:], xn_f[:], 1.0, xn_f[:],
                                           ALU.bypass, ALU.mult).then_inc(sem_v, 1)
                    v.tensor_scalar(msk8[:], hi2_t[:], thr_sb[:, 0:1], None,
                                    ALU.is_gt).then_inc(sem_v, 1)
                    v.tensor_scalar(accw[:, p], xn_f[:], cpx_sb[:, 0:1], cnx_sb[:, 0:1],
                                    ALU.min, ALU.max).then_inc(sem_v, 1)
                emit("vector", xn1)
                St.v += 3
                need_v = St.v

                def q1s(s, tt=tt, b=b, c=c, p=p, need_v=need_v,
                        need_t=(tr2_done.get(tt - 1) if tt >= 1 else
                                (w2_end_t if b == 0 else last_pv_t)),
                        need_pk=(acc_parked.get((b, tt - 2))),
                        need_y=(out_done.get((b - 1, NT - 2 + tt))
                                if (b >= 1 and tt < 2) else None)):
                    s.wait_ge(sem_v, need_v)
                    if need_t:
                        s.wait_ge(sem_t, need_t)
                    if need_pk is not None:
                        s.wait_ge(sem_pk[p], need_pk)
                    if need_y is not None:
                        s.wait_ge(sem_y[p], need_y)
                    s.activation(lo8_t[:], accw[:, p], AF.Copy, scale=sx_sb[:, 0:1]).then_inc(sem_s, 1)
                    s.activation(hi8_t[:], xn_f[:], AF.Copy, scale=sxh_sb[:, 0:1]).then_inc(sem_s, 1)
                    s.activation(xn_bf(), xn_f[:], AF.Copy).then_inc(sem_s, 1)
                    s.activation(accw[:, p], xw[:, p], AF.Identity,
                                 scale=s1p_st[:, c], bias=s2p_st[:, c]).then_inc(sem_s, 1)
                emit("scalar", q1s)
                St.s += 4
                acc0_s = St.s
                need_s = St.s

                def q1v(v, need_s=need_s,
                        need_t=(tr2_done.get(tt - 1) if tt >= 1 else
                                (w2_end_t if b == 0 else last_pv_t))):
                    v.wait_ge(sem_s, need_s)
                    if need_t:
                        v.wait_ge(sem_t, need_t)
                    v.tensor_scalar(hi2_t[:], hi8_t[:], 2.0, None, ALU.mult).then_inc(sem_v, 1)
                    v.select(q_bf(), msk8[:], hi2_t[:], lo8_t[:], add_drain=True).then_inc(sem_v, 1)
                emit("vector", q1v)
                St.v += 2
                need_v = St.v

                def tr1(t, need_v=need_v, need_pstr=St.pstr_evac[0]):
                    t.wait_ge(sem_v, need_v)
                    t.wait_ge(sem_v, need_pstr)
                    ins = None
                    for blk in range(8):
                        ins = t.transpose(ps_tr[:, blk * 128:(blk + 1) * 128],
                                          xn_bf()[:, blk * 128:(blk + 1) * 128], ident_sb[:])
                    ins.then_inc(sem_t, 1)
                emit("tensor", tr1)
                St.t += 1
                need_t = St.t

                def ev1(v, rs=rs, need_t=need_t):
                    v.wait_ge(sem_t, need_t)
                    for blk in range(8):
                        v.tensor_copy(xnT(blk)[:, rs],
                                      ps_tr[:, blk * 128:(blk + 1) * 128]).then_inc(sem_v, 1)
                emit("vector", ev1)
                St.v += 8
                St.pstr_evac[0] = St.v

                def tr2(t, need_v=need_v, need_pstr=St.pstr_evac[1]):
                    t.wait_ge(sem_v, need_v)
                    if need_pstr:
                        t.wait_ge(sem_v, need_pstr)
                    ins = None
                    for blk in range(8):
                        ins = t.transpose(ps_tr2[:, blk * 128:(blk + 1) * 128],
                                          q_bf()[:, blk * 128:(blk + 1) * 128], ident_sb[:])
                    ins.then_inc(sem_t, 1)
                emit("tensor", tr2)
                St.t += 1
                tr2_done[tt] = St.t
                need_t = St.t

                def ev2(v, b=b, tt=tt, need_t=need_t, need_qp=qp_final):
                    v.wait_ge(sem_t, need_t)
                    if b == 0 and tt == 0:
                        v.wait_ge(sem_qp, need_qp)
                    v.tensor_copy(xqT_st[:], ps_tr2[:]).then_inc(sem_v, 1)
                emit("vector", ev2)
                St.v += 1
                St.pstr_evac[1] = St.v
                xq_ready_v = St.v

                for oc, bank in ((0, BANKS[(2 * tt + 0) % 4]), (1, BANKS[(2 * tt + 1) % 4])):
                    osl = slice(oc * 512, (oc + 1) * 512)

                    def ffn_mm(t, oc=oc, osl=osl, bank=bank, tt=tt,
                               need_v=xq_ready_v, w_in=w8T_in,
                               bank_free=St.bank_evac[bank]):
                        t.wait_ge(sem_v, need_v)
                        if tt == 0 and oc == 0:
                            t.wait_ge(sem_w, w_in)
                        if bank_free:
                            t.wait_ge(sem_pvv if bank_free[0] == "v" else sem_pvs,
                                      bank_free[1])
                        for dcc in range(DC):
                            t.matmul(PS[bank][:, :],
                                     xqT_st[:, dcc * 128:(dcc + 1) * 128],
                                     wbf(dcc)[:, osl],
                                     start=(dcc == 0), stop=False)
                        t.matmul(PS[bank][:, :], ones_sb[0:1, :], bf_rs[0:1, osl],
                                 start=False, stop=True).then_inc(sem_t, 1)
                    emit("tensor", ffn_mm)
                    St.t += 1
                    last_wbf_reader_t = St.t
                    need_t = St.t

                    def ffn_ev(v, osl=osl, bank=bank, p=p,
                               need_t=need_t, need_s=acc0_s):
                        v.wait_ge(sem_t, need_t)
                        v.wait_ge(sem_s, need_s)
                        v.scalar_tensor_tensor(accw[:, p, osl.start:osl.stop],
                                               PS[bank][:, :], scinv_sb[:, 0:1],
                                               accw[:, p, osl.start:osl.stop],
                                               ALU.mult, ALU.add).then_inc(sem_v, 1)
                        v.engine_nop().then_inc(sem_pvv, 1)
                    emit("vector", ffn_ev)
                    St.v += 1
                    St.pv_v += 1
                    St.bank_evac[bank] = ("v", St.pv_v)

                def park1(g, b=b, rs=rs, p=p, need_v=St.v):
                    g.wait_ge(sem_v, need_v)
                    g.dma_start(out=acc3[b, rs, :], in_=accw[:, p]).then_inc(sem_pk[p], 16)
                emit("gpsimd", park1)
                St.pk[p] += 16
                acc_parked[(b, tt)] = St.pk[p]

            xnT_all_v = St.v

            for name, wi, biascol in (("q", 0, bq_sb), ("k", 1, bk_sb), ("v", 2, None)):
                for tc in range(DC):
                    def ldq(sync, wi=wi, tc=tc, name=name, b=b,
                            need_t=last_wbf_reader_t, wpf=wp_final):
                        if tc == 0:
                            sync.wait_ge(sem_t, need_t)
                            if name == "q" and b == 0:
                                sync.wait_ge(sem_wp[0], wpf[0])
                                sync.wait_ge(sem_wp[1], wpf[1])
                        sync.dma_start(out=wbf(tc),
                                       in_=wqkv_bf.ap()[wi, tc * 128:(tc + 1) * 128, :]).then_inc(sem_w, 16)
                    emit("sync", ldq)
                    St.w += 16
                w_in = St.w

                if name in ("q", "k"):
                    dst = qt if name == "q" else kt
                    for kc in range(DC):
                        for ich in range(NIC):
                            isl = slice(ich * 512, (ich + 1) * 512)
                            bank = BANKS[(kc * NIC + ich) % 4]

                            def qk_mm(t, kc=kc, isl=isl, bank=bank, w_in=w_in,
                                      first=(kc == 0 and ich == 0),
                                      need_v=xnT_all_v,
                                      bank_free=St.bank_evac[bank]):
                                if first:
                                    t.wait_ge(sem_w, w_in)
                                    t.wait_ge(sem_v, need_v)
                                if bank_free:
                                    t.wait_ge(sem_pvv if bank_free[0] == "v" else sem_pvs,
                                              bank_free[1])
                                ins = None
                                for dcc in range(DC):
                                    ins = t.matmul(PS[bank][:, :],
                                                   wbf(dcc)[:, kc * 128:(kc + 1) * 128],
                                                   xnT(dcc)[:, isl],
                                                   start=(dcc == 0), stop=(dcc == DC - 1))
                                ins.then_inc(sem_t, 1)
                            emit("tensor", qk_mm)
                            St.t += 1
                            last_wbf_reader_t = St.t
                            need_t = St.t

                            def qk_ev(s, kc=kc, isl=isl, bank=bank, dst=dst,
                                      biascol=biascol, need_t=need_t):
                                s.wait_ge(sem_t, need_t)
                                s.activation(dst(kc)[:, isl], PS[bank][:, :], AF.Identity,
                                             bias=biascol[:, kc:kc + 1]).then_inc(sem_s, 1)
                                s.nop().then_inc(sem_pvs, 1)
                            emit("scalar", qk_ev)
                            St.s += 1
                            St.pv_s += 1
                            St.bank_evac[bank] = ("s", St.pv_s)
                else:
                    for jt in range(NT):
                        js = slice(jt * 128, (jt + 1) * 128)
                        for vc in range(2):
                            vsl = slice(vc * 512, (vc + 1) * 512)
                            bank = BANKS[(jt * 2 + vc) % 4]

                            def v_mm(t, js=js, vsl=vsl, bank=bank, w_in=w_in,
                                     first=(jt == 0 and vc == 0),
                                     bank_free=St.bank_evac[bank]):
                                if first:
                                    t.wait_ge(sem_w, w_in)
                                if bank_free:
                                    t.wait_ge(sem_pvv if bank_free[0] == "v" else sem_pvs,
                                              bank_free[1])
                                for dcc in range(DC):
                                    t.matmul(PS[bank][:, :],
                                             xnT(dcc)[:, js],
                                             wbf(dcc)[:, vsl],
                                             start=(dcc == 0), stop=False)
                                t.matmul(PS[bank][:, :], ones_sb[0:1, :], bv_row[0:1, vsl],
                                         start=False, stop=True).then_inc(sem_t, 1)
                            emit("tensor", v_mm)
                            St.t += 1
                            last_wbf_reader_t = St.t
                            need_t = St.t

                            def v_ev(s, jt=jt, vsl=vsl, bank=bank, need_t=need_t):
                                s.wait_ge(sem_t, need_t)
                                s.activation(vt(jt)[:, vsl], PS[bank][:, :], AF.Copy).then_inc(sem_s, 1)
                                s.nop().then_inc(sem_pvs, 1)
                            emit("scalar", v_ev)
                            St.s += 1
                            St.pv_s += 1
                            St.bank_evac[bank] = ("s", St.pv_s)

            for jt in range(NT):
                j0 = jt * 128
                js = slice(j0, j0 + 128)
                vcs = valid_chunks(jt)
                for n_i, ic in enumerate(vcs):
                    isl = slice(ic * 512, (ic + 1) * 512)
                    bank = BANKS[sum(len(valid_chunks(j)) for j in range(jt)) % 4 if False else (jt * NIC + ic) % 4]

                    def s_mm(t, js=js, isl=isl, bank=bank,
                             bank_free=St.bank_evac[bank]):
                        if bank_free:
                            t.wait_ge(sem_pvv if bank_free[0] == "v" else sem_pvs,
                                      bank_free[1])
                        ins = None
                        for kc in range(DC):
                            ins = t.matmul(PS[bank][:, :],
                                           kt(kc)[:, js],
                                           qt(kc)[:, isl],
                                           start=(kc == 0), stop=(kc == DC - 1))
                        ins.then_inc(sem_t, 1)
                    emit("tensor", s_mm)
                    St.t += 1
                    need_t = St.t

                    delta = j0 - ic * 512
                    partial = delta > -127

                    mrun_v = St.v + (3 if partial else 2)
                    def s_st(v, n_i=n_i, bank=bank, partial=partial, delta=delta,
                             need_t=need_t, mrun_v=mrun_v):
                        v.wait_ge(sem_t, need_t)
                        if partial:
                            off = 512 - delta
                            v.tensor_add(PS[bank][:, :], PS[bank][:, :],
                                         band_sb[:, off:off + 512]).then_inc(sem_v, 1)
                        v.reduce_max(tinyA[:, n_i:n_i + 1], PS[bank][:, :], axis=AX.X).then_inc(sem_v, 1)
                        if n_i == 0:
                            v.tensor_copy(mrun[:], tinyA[:, 0:1]).then_inc(sem_v, 1)
                        else:
                            v.tensor_max(mrun[:], mrun[:], tinyA[:, n_i:n_i + 1]).then_inc(sem_v, 1)
                        v.tensor_copy(msave[:, n_i:n_i + 1], mrun[:]).then_inc(sem_v, 1)
                        v.wait_ge(sem_v, mrun_v)
                        v.tensor_scalar(mnegs[:, n_i:n_i + 1], mrun[:], -1.0, None,
                                        ALU.mult).then_inc(sem_v, 1)
                    emit("vector", s_st)
                    St.v += 5 if partial else 4
                    need_v = St.v

                    def s_exp(s, jt=jt, ic=ic, bank=bank, n_i=n_i, need_v=need_v):
                        s.wait_ge(sem_v, need_v)
                        s.activation(e_chunk(jt, ic), PS[bank][:, :], AF.Exp,
                                     bias=mnegs[:, n_i:n_i + 1],
                                     accum_out=ssave[:, n_i:n_i + 1]).then_inc(sem_s, 1)
                        s.nop().then_inc(sem_pvs, 1)
                    emit("scalar", s_exp)
                    St.s += 1
                    St.pv_s += 1
                    St.bank_evac[bank] = ("s", St.pv_s)
                exp_done_s = St.s
                nv = len(vcs)

                def fin_v(v, nv=nv, need_s=exp_done_s, base=St.v):
                    v.wait_ge(sem_s, need_s)
                    k = [base]

                    def w(ins):
                        k[0] += 1
                        ins.then_inc(sem_v, 1)
                        v.wait_ge(sem_v, k[0])
                    w(v.reduce_max(mrun[:], tinyA[:, 0:nv], axis=AX.X))
                    w(v.tensor_scalar(tinyA[:, 0:nv], tinyA[:, 0:nv], mrun[:, 0:1], None,
                                      ALU.subtract))
                emit("vector", fin_v)
                St.v += 2
                need_v = St.v

                def fin_s(s, nv=nv, need_v=need_v):
                    s.wait_ge(sem_v, need_v)
                    s.activation(tinyB[:, 0:nv], tinyA[:, 0:nv], AF.Exp).then_inc(sem_s, 1)
                emit("scalar", fin_s)
                St.s += 1
                need_s = St.s

                def fin_v2(v, jt=jt, nv=nv, need_s=need_s):
                    v.wait_ge(sem_s, need_s)
                    v.tensor_tensor(tinyA[:, 0:nv], tinyB[:, 0:nv], ssave[:, 0:nv], ALU.mult).then_inc(sem_v, 1)
                    v.reduce_sum(lsum[:], tinyA[:, 0:nv], axis=AX.X).then_inc(sem_v, 1)
                    v.reciprocal(lsum[:], lsum[:]).then_inc(sem_v, 1)
                    v.tensor_scalar(lsum[:], lsum[:], 1.0 / 32.0, None, ALU.mult).then_inc(sem_v, 1)
                    v.tensor_scalar(gall[:, jt, 0:nv], tinyB[:, 0:nv], lsum[:, 0:1], None,
                                    ALU.mult).then_inc(sem_v, 1)
                emit("vector", fin_v2)
                St.v += 5

            pv_mm_t = {}
            for it in range(NT):
                i0 = it * 128
                icov = i0 // 512
                p = it % 2

                def acc_ld(g, b=b, rs=slice(i0, i0 + 128), it=it, p=p,
                           need_y=out_done.get((b, it - 2)),
                           need_pk=(acc_parked.get((b, NT - 2 + it)) if it < 2 else None)):
                    if need_y is not None:
                        g.wait_ge(sem_y[p], need_y)
                    if need_pk is not None:
                        g.wait_ge(sem_pk[p], need_pk)
                    g.dma_start(out=accw[:, p], in_=acc3[b, rs, :]).then_inc(sem_acc[p], 16)
                emit("gpsimd", acc_ld)
                St.acc[p] += 16
                acc_in = St.acc[p]

                def resc(v, it=it, i0=i0, icov=icov,
                         need_t=pv_mm_t.get(it - 1) if it >= 1 else None):
                    if need_t is not None:
                        v.wait_ge(sem_t, need_t)
                    for jt in range(it + 1):
                        gcol = icov - valid_chunks(jt)[0]
                        v.tensor_scalar(pstage(jt),
                                        e_cols(jt, i0, 128),
                                        gall[:, jt, gcol:gcol + 1], None,
                                        ALU.mult).then_inc(sem_v, 1)
                emit("vector", resc)
                St.v += it + 1
                last_resc_v = St.v
                need_v = St.v

                for vc, bank in ((0, BANKS[(2 * it + 0) % 4]), (1, BANKS[(2 * it + 1) % 4])):
                    vsl = slice(vc * 512, (vc + 1) * 512)

                    def pv_mm(t, it=it, vsl=vsl, bank=bank, need_v=need_v,
                              bank_free=St.bank_evac[bank]):
                        t.wait_ge(sem_v, need_v)
                        if bank_free:
                            t.wait_ge(sem_pvv if bank_free[0] == "v" else sem_pvs,
                                      bank_free[1])
                        ins = None
                        for jt in range(it + 1):
                            ins = t.matmul(PS[bank][:, :],
                                           pstage(jt),
                                           vt(jt)[:, vsl],
                                           start=(jt == 0), stop=(jt == it))
                        ins.then_inc(sem_t, 1)
                    emit("tensor", pv_mm)
                    St.t += 1
                    need_t_mm = St.t

                    def merge(v, vsl=vsl, bank=bank, p=p, need_t=need_t_mm,
                              need_acc=acc_in):
                        v.wait_ge(sem_t, need_t)
                        v.wait_ge(sem_acc[p], need_acc)
                        v.tensor_add(accw[:, p, vsl.start:vsl.stop],
                                     accw[:, p, vsl.start:vsl.stop],
                                     PS[bank][:, :]).then_inc(sem_v, 1)
                        v.engine_nop().then_inc(sem_pvv, 1)
                    emit("vector", merge)
                    St.v += 1
                    St.pv_v += 1
                    St.bank_evac[bank] = ("v", St.pv_v)
                pv_mm_t[it] = St.t
                last_pv_t = St.t
                merged_v = St.v

                def out_wr(sync, b=b, rs=slice(i0, i0 + 128), p=p, need_v=merged_v):
                    sync.wait_ge(sem_v, need_v)
                    sync.dma_start(out=y3[b, rs, :], in_=accw[:, p]).then_inc(sem_y[p], 16)
                emit("sync", out_wr)
                St.y[p] += 16
                out_done[(b, it)] = St.y[p]

        yf = (St.y[0], St.y[1])

        def fin(g, yf=yf):
            g.wait_ge(sem_y[0], yf[0])
            g.wait_ge(sem_y[1], yf[1])
        emit("gpsimd", fin)

        class _SerIns:
            def __init__(self, ins, proxy):
                self._ins = ins
                self._proxy = proxy

            def then_inc(self, sem, n):
                self._ins.then_inc(sem, n)
                if sem is self._proxy._own:
                    self._proxy._n += n
                return self

        class _SerEng:
            _OPS = frozenset((
                "tensor_scalar", "tensor_tensor", "tensor_add", "tensor_sub",
                "tensor_max", "tensor_copy", "reduce_max", "reduce_sum",
                "tensor_reduce", "reciprocal", "select", "scalar_tensor_tensor",
                "engine_nop", "memset", "memzero", "activation", "nop",
                "tensor_single_scalar", "copy", "add", "mul", "square", "sqrt",
            ))

            def __init__(self, eng, own_sem):
                self._eng = eng
                self._own = own_sem
                self._n = 0

            def wait_ge(self, sem, val):
                return self._eng.wait_ge(sem, val)

            def __getattr__(self, name):
                target = getattr(self._eng, name)
                if name in _SerEng._OPS:
                    def call(*a, _t=target, **k):
                        if self._n:
                            self._eng.wait_ge(self._own, self._n)
                        return _SerIns(_t(*a, **k), self)
                    return call
                return target

        @block.sync
        def _(sync):
            for fn in prog["sync"]:
                fn(sync)

        @block.gpsimd
        def _(g):
            for fn in prog["gpsimd"]:
                fn(g)

        final_v = St.v
        final_s = St.s

        @block.vector
        def _(v):
            vw = _SerEng(v, sem_v)
            for fn in prog["vector"]:
                fn(vw)
            assert vw._n == final_v, f"vector inc count {vw._n} != bookkeeping {final_v}"

        @block.scalar
        def _(s):
            sw = _SerEng(s, sem_s)
            for fn in prog["scalar"]:
                fn(sw)
            assert sw._n == final_s, f"scalar inc count {sw._n} != bookkeeping {final_s}"

        @block.tensor
        def _(t):
            for fn in prog["tensor"]:
                fn(t)

    nc.finalize()
    return nc


_NC_CACHE = None


def kernel(L0_attn_minibatch, L0_attn_Wq, L0_attn_bq, L0_attn_Wk, L0_attn_bk,
           L0_attn_Wv, L0_attn_bv, L0_ffn_weight, L0_ffn_bias):
    global _NC_CACHE
    x = np.ascontiguousarray(L0_attn_minibatch, dtype=np.float32)
    shards = [np.ascontiguousarray(x[2 * c:2 * c + 2]) for c in range(8)]
    common = {
        "wq": np.ascontiguousarray(L0_attn_Wq, dtype=np.float32),
        "wk": np.ascontiguousarray(L0_attn_Wk, dtype=np.float32),
        "wv": np.ascontiguousarray(L0_attn_Wv, dtype=np.float32),
        "wf": np.ascontiguousarray(L0_ffn_weight, dtype=np.float32),
        "bq": np.ascontiguousarray(L0_attn_bq, dtype=np.float32).reshape(1, D),
        "bk": np.ascontiguousarray(L0_attn_bk, dtype=np.float32).reshape(1, D),
        "bv": np.ascontiguousarray(L0_attn_bv, dtype=np.float32).reshape(1, D),
        "bf": np.ascontiguousarray(L0_ffn_bias, dtype=np.float32).reshape(1, D),
    }
    if _NC_CACHE is None:
        _NC_CACHE = build_program()
    nc = _NC_CACHE
    in_maps = [{"x": shards[c], "xf": x, **common} for c in range(8)]
    res = run_bass_kernel_spmd(nc, in_maps, core_ids=list(range(8)))
    return np.concatenate([res.results[c]["y"] for c in range(8)], axis=0)


# version 16 (final)
# speedup vs baseline: 1.0072x; 1.0072x over previous
import numpy as np
import concourse.bass as bass
import concourse.mybir as mybir
from concourse.bacc import Bacc
from concourse.bass_utils import run_bass_kernel_spmd

F32 = mybir.dt.float32
BF16 = mybir.dt.bfloat16
FP8 = mybir.dt.float8e4
U8 = mybir.dt.uint8
AF = mybir.ActivationFunctionType
ALU = mybir.AluOpType
AX = mybir.AxisListType

B_LOC = 2
T = 2048
D = 1024
NT = T // 128
DC = D // 128
NIC = T // 512
NEG = -1.0e30
MASK_W = 1152


def valid_chunks(jt):
    j0 = jt * 128
    return [ic for ic in range(NIC) if (ic + 1) * 512 - 1 >= j0]


E_OFF = {}
_off = 0
for _jt in range(NT):
    E_OFF[_jt] = _off
    _off += len(valid_chunks(_jt)) * 512
E_TOTAL = _off


def build_program():
    import ml_dtypes
    nc = Bacc("TRN2", target_bir_lowering=False)

    x_h = nc.dram_tensor("x", [B_LOC, T, D], F32, kind="ExternalInput")
    xf_h = nc.dram_tensor("xf", [16, T, D], F32, kind="ExternalInput")
    wq_h = nc.dram_tensor("wq", [D, D], F32, kind="ExternalInput")
    wk_h = nc.dram_tensor("wk", [D, D], F32, kind="ExternalInput")
    wv_h = nc.dram_tensor("wv", [D, D], F32, kind="ExternalInput")
    wf_h = nc.dram_tensor("wf", [D, D], F32, kind="ExternalInput")
    bq_h = nc.dram_tensor("bq", [1, D], F32, kind="ExternalInput")
    bk_h = nc.dram_tensor("bk", [1, D], F32, kind="ExternalInput")
    bv_h = nc.dram_tensor("bv", [1, D], F32, kind="ExternalInput")
    bf_h = nc.dram_tensor("bf", [1, D], F32, kind="ExternalInput")
    y_h = nc.dram_tensor("y", [B_LOC, T, D], F32, kind="ExternalOutput")

    wqkv_bf = nc.dram_tensor("wqkv_bf", [3, D, D], BF16, kind="Internal")
    w8T_d = nc.dram_tensor("w8T_d", [D, D], BF16, kind="Internal")
    acc_d = nc.dram_tensor("acc_d", [B_LOC, T, D], F32, kind="Internal")
    bounce = nc.dram_tensor("bounce", [1, 16], F32, kind="Internal")

    ident_h = nc.inline_tensor(np.eye(128, dtype=ml_dtypes.bfloat16), name="ident")
    uu = np.arange(MASK_W)[None, :]
    pp = np.arange(128)[:, None]
    band_np = np.where(uu - pp >= 512, 0.0, NEG).astype(ml_dtypes.bfloat16)
    band_h = nc.inline_tensor(band_np, name="band")
    ones_h = nc.inline_tensor(np.ones((1, 128), dtype=ml_dtypes.bfloat16), name="ones")

    x3 = x_h.ap()
    xf3 = xf_h.ap()
    y3 = y_h.ap()
    acc3 = acc_d.ap()

    from contextlib import ExitStack
    with ExitStack() as _ctx:
        E = _ctx.enter_context
        ident_sb = E(nc.sbuf_tensor("ident_sb", [128, 128], BF16))
        band_sb = E(nc.sbuf_tensor("band_sb", [128, MASK_W], BF16))
        ones_sb = E(nc.sbuf_tensor("ones_sb", [1, 128], BF16))
        bq_sb = E(nc.sbuf_tensor("bq_sb", [128, DC], F32))
        bk_sb = E(nc.sbuf_tensor("bk_sb", [128, DC], F32))
        bv_row = E(nc.sbuf_tensor("bv_row", [1, D], BF16))
        bf_row = E(nc.sbuf_tensor("bf_row", [1, D], BF16))
        mneg_st = E(nc.sbuf_tensor("mneg_st", [128, 2 * NT], F32))
        r_st = E(nc.sbuf_tensor("r_st", [128, 2 * NT], F32))
        s1p_st = E(nc.sbuf_tensor("s1p_st", [128, 2 * NT], F32))
        s2p_st = E(nc.sbuf_tensor("s2p_st", [128, 2 * NT], F32))
        mx_st = E(nc.sbuf_tensor("mx_st", [128, 2 * NT], F32))
        mn_st = E(nc.sbuf_tensor("mn_st", [128, 2 * NT], F32))
        tmp_c = E(nc.sbuf_tensor("tmp_c", [128, 2 * NT], F32))
        amax_acc = E(nc.sbuf_tensor("amax_acc", [128, 1], F32))
        amax_bc = E(nc.sbuf_tensor("amax_bc", [128, 1], F32))
        fmax = E(nc.sbuf_tensor("fmax", [128, 256], F32))
        fmin = E(nc.sbuf_tensor("fmin", [128, 256], F32))
        fsum = E(nc.sbuf_tensor("fsum", [128, 256], F32))
        fsq = E(nc.sbuf_tensor("fsq", [128, 256], F32))
        tmp_f = E(nc.sbuf_tensor("tmp_f", [128, 256], F32))
        tmp_g = E(nc.sbuf_tensor("tmp_g", [128, 256], F32))
        sx_sb = E(nc.sbuf_tensor("sx_sb", [128, 1], F32))
        sxh_sb = E(nc.sbuf_tensor("sxh_sb", [128, 1], F32))
        thr_sb = E(nc.sbuf_tensor("thr_sb", [128, 1], F32))
        row_sc = E(nc.sbuf_tensor("row_sc", [1, 128], F32))
        swq_sb = E(nc.sbuf_tensor("swq_sb", [128, 1], F32))
        swqh_sb = E(nc.sbuf_tensor("swqh_sb", [128, 1], F32))
        thrw_sb = E(nc.sbuf_tensor("thrw_sb", [128, 1], F32))
        t1_sb = E(nc.sbuf_tensor("t1_sb", [128, 1], F32))
        t2_sb = E(nc.sbuf_tensor("t2_sb", [128, 1], F32))
        cpx_sb = E(nc.sbuf_tensor("cpx_sb", [128, 1], F32))
        cnx_sb = E(nc.sbuf_tensor("cnx_sb", [128, 1], F32))
        cpw_sb = E(nc.sbuf_tensor("cpw_sb", [128, 1], F32))
        cnw_sb = E(nc.sbuf_tensor("cnw_sb", [128, 1], F32))
        sprod_sb = E(nc.sbuf_tensor("sprod_sb", [128, 1], F32))
        scinv_sb = E(nc.sbuf_tensor("scinv_sb", [128, 1], F32))
        bf_rs = E(nc.sbuf_tensor("bf_rs", [1, D], BF16))
        xw = E(nc.sbuf_tensor("xw", [128, 3, D], F32))
        xn_f = E(nc.sbuf_tensor("xn_f", [128, D], F32))
        accw = E(nc.sbuf_tensor("accw", [128, 2, D], F32))
        msk8 = E(nc.sbuf_tensor("msk8", [128, D], U8))
        lo8_t = E(nc.sbuf_tensor("lo8", [128, D], FP8))
        hi8_t = E(nc.sbuf_tensor("hi8", [128, D], FP8))
        hi2_t = E(nc.sbuf_tensor("hi2", [128, D], BF16))
        xqT_st = E(nc.sbuf_tensor("xqT_st", [128, D], BF16))
        bscr = E(nc.sbuf_tensor("bscr", [128, T], BF16))
        ereg = E(nc.sbuf_tensor("ereg", [128, 24576], BF16))
        qt_r = E(nc.sbuf_tensor("qt_r", [128, DC, T], BF16))
        kt_r = E(nc.sbuf_tensor("kt_r", [128, DC, T], BF16))
        v_r = E(nc.sbuf_tensor("v_r", [128, NT, D], BF16))
        mrun = E(nc.sbuf_tensor("mrun", [128, 1], F32))
        mnegs = E(nc.sbuf_tensor("mnegs", [128, NIC], F32))
        msave = E(nc.sbuf_tensor("msave", [128, NIC], F32))
        ssave = E(nc.sbuf_tensor("ssave", [128, NIC], F32))
        gall = E(nc.sbuf_tensor("gall", [128, NT, NIC], F32))
        tinyA = E(nc.sbuf_tensor("tinyA", [128, NIC], F32))
        tinyB = E(nc.sbuf_tensor("tinyB", [128, NIC], F32))
        lsum = E(nc.sbuf_tensor("lsum", [128, 1], F32))
        ps_a = E(nc.psum_tensor("ps_a", [128, 512], F32))
        ps_b = E(nc.psum_tensor("ps_b", [128, 512], F32))
        ps_c = E(nc.psum_tensor("ps_c", [128, 512], F32))
        ps_d = E(nc.psum_tensor("ps_d", [128, 512], F32))
        ps_tr = E(nc.psum_tensor("ps_tr", [128, 1024], BF16))
        ps_tr2 = E(nc.psum_tensor("ps_tr2", [128, 1024], BF16))
        PSTR = (ps_tr, ps_tr2)
        sem_cst = E(nc.semaphore("sem_cst"))
        sem_x = [E(nc.semaphore("sem_x0")), E(nc.semaphore("sem_x1")),
                 E(nc.semaphore("sem_x2"))]
        sem_f1 = E(nc.semaphore("sem_f1"))
        sem_f2 = E(nc.semaphore("sem_f2"))
        sem_w = E(nc.semaphore("sem_w"))
        sem_wl = [E(nc.semaphore("sem_wl0")), E(nc.semaphore("sem_wl1"))]
        sem_wp = [E(nc.semaphore("sem_wp0")), E(nc.semaphore("sem_wp1"))]
        sem_qp = E(nc.semaphore("sem_qp"))
        sem_pk = [E(nc.semaphore("sem_pk0")), E(nc.semaphore("sem_pk1"))]
        sem_acc = [E(nc.semaphore("sem_acc0")), E(nc.semaphore("sem_acc1"))]
        sem_y = [E(nc.semaphore("sem_y0")), E(nc.semaphore("sem_y1"))]
        sem_sg = E(nc.semaphore("sem_sg"))
        sem_v = E(nc.semaphore("sem_v"))
        sem_s = E(nc.semaphore("sem_s"))
        sem_t = E(nc.semaphore("sem_t"))
        sem_pvv = E(nc.semaphore("sem_pvv"))
        sem_pvs = E(nc.semaphore("sem_pvs"))
        block = E(nc.Block())

        PS = {"a": ps_a, "b": ps_b, "c": ps_c, "d": ps_d}
        BANKS = ("a", "b", "c", "d")

        def sqw():
            return xn_f[:]

        def wld(i):
            return xw[:, i]

        def wcast(i):
            return bscr[:, i * D:(i + 1) * D]

        def xnT(dcc):
            return ereg[:, dcc * T:(dcc + 1) * T]

        def wbf(dcc):
            return ereg[:, DC * T + dcc * D: DC * T + (dcc + 1) * D]

        def e_chunk(jt, ic):
            fic = valid_chunks(jt)[0]
            o = E_OFF[jt] + (ic - fic) * 512
            return ereg[:, o:o + 512]

        def e_cols(jt, i0, w):
            ic = i0 // 512
            fic = valid_chunks(jt)[0]
            o = E_OFF[jt] + (ic - fic) * 512 + (i0 - ic * 512)
            return ereg[:, o:o + w]

        def q_bf():
            return bscr[:, 0:D]

        def xn_bf():
            return bscr[:, D:2 * D]

        def pstage(jt):
            return bscr[:, jt * 128:(jt + 1) * 128]

        def qt(kc):
            return qt_r[:, kc]

        def kt(kc):
            return kt_r[:, kc]

        def vt(jt):
            return v_r[:, jt]

        class St:
            cst = 0
            x = [0, 0, 0]
            f1 = 0
            f2 = 0
            w = 0
            wl = [0, 0]
            wp = [0, 0]
            qp = 0
            pk = [0, 0]
            acc = [0, 0]
            y = [0, 0]
            sg = 0
            v = 0
            s = 0
            t = 0
            pv_v = 0
            pv_s = 0
            bank_evac = {"a": None, "b": None, "c": None, "d": None}
            pstr_evac = [0, 0]

        prog = {k: [] for k in ("sync", "gpsimd", "vector", "scalar", "tensor")}

        _ENG_SEM = {"vector": (sem_v, "v"), "scalar": (sem_s, "s"), "tensor": (sem_t, "t")}

        def emit(engine, fn):
            if engine in _ENG_SEM:
                sem, attr = _ENG_SEM[engine]
                cnt = getattr(St, attr)

                def wrapped(eng, fn=fn, sem=sem, cnt=cnt):
                    if cnt:
                        eng.wait_ge(sem, cnt)
                    fn(eng)
                prog[engine].append(wrapped)
            else:
                prog[engine].append(fn)

        def c_loads(sync):
            sync.dma_start(out=ident_sb[:], in_=ident_h.ap()).then_inc(sem_cst, 16)
            sync.dma_start(out=band_sb[:], in_=band_h.ap()).then_inc(sem_cst, 16)
            sync.dma_start(out=ones_sb[:], in_=ones_h.ap()).then_inc(sem_cst, 16)
            for kc in range(DC):
                sync.dma_start(out=bq_sb[:, kc:kc + 1],
                               in_=bq_h.ap()[0:1, kc * 128:(kc + 1) * 128].rearrange("a b -> (a b)").unsqueeze(1)).then_inc(sem_cst, 16)
                sync.dma_start(out=bk_sb[:, kc:kc + 1],
                               in_=bk_h.ap()[0:1, kc * 128:(kc + 1) * 128].rearrange("a b -> (a b)").unsqueeze(1)).then_inc(sem_cst, 16)
            sync.dma_start(out=xn_f[0:1, :], in_=bv_h.ap()).then_inc(sem_cst, 16)
        emit("sync", c_loads)
        St.cst += (4 + 2 * DC) * 16
        cst1 = St.cst

        def c_cast1(v, need=cst1):
            v.wait_ge(sem_cst, need)
            v.tensor_copy(bv_row[:], xn_f[0:1, :]).then_inc(sem_v, 1)
        emit("vector", c_cast1)
        St.v += 1
        c_cast1_v = St.v

        def c_load2(sync, need=c_cast1_v):
            sync.wait_ge(sem_v, need)
            sync.dma_start(out=xn_f[0:1, :], in_=bf_h.ap()).then_inc(sem_cst, 16)
        emit("sync", c_load2)
        St.cst += 16
        cst2 = St.cst

        def c_cast2(v, need=cst2):
            v.wait_ge(sem_cst, need)
            v.tensor_copy(bf_row[:], xn_f[0:1, :]).then_inc(sem_v, 1)
        emit("vector", c_cast2)
        St.v += 1
        c_cast2_v = St.v

        cast_done = {}
        for wi, wsrc in enumerate((wq_h, wk_h, wv_h)):
            for tc in range(DC):
                k = wi * DC + tc
                sl = slice(tc * 128, (tc + 1) * 128)
                p = k % 2

                def ld(sync, wsrc=wsrc, sl=sl, k=k, p=p):
                    if k >= 2:
                        sync.wait_ge(sem_v, cast_done[k - 2])
                    else:
                        sync.wait_ge(sem_v, c_cast1_v)
                    sync.dma_start(out=wld(p), in_=wsrc.ap()[sl, :]).then_inc(sem_wl[p], 16)
                emit("sync", ld)
                St.wl[p] += 16
                need_wl = St.wl[p]

                def cast(v, k=k, p=p, need_wl=need_wl, need_wp=St.wp[p]):
                    v.wait_ge(sem_wl[p], need_wl)
                    if need_wp:
                        v.wait_ge(sem_wp[p], need_wp)
                    v.tensor_copy(wcast(p), wld(p)).then_inc(sem_v, 1)
                emit("vector", cast)
                St.v += 1
                cast_done[k] = St.v
                need_v = St.v

                def park(g, sl=sl, k=k, wi=wi, p=p, need_v=need_v):
                    g.wait_ge(sem_v, need_v)
                    g.dma_start(out=wqkv_bf.ap()[wi, sl, :], in_=wcast(p)).then_inc(sem_wp[p], 16)
                emit("gpsimd", park)
                St.wp[p] += 16
        wp_final = (St.wp[0], St.wp[1])

        wamax_done = {}
        for tc in range(DC):
            sl = slice(tc * 128, (tc + 1) * 128)
            p = tc % 2

            def ldf(sync, sl=sl, tc=tc, p=p):
                if tc >= 2:
                    sync.wait_ge(sem_v, wamax_done[tc - 2])
                else:
                    sync.wait_ge(sem_v, cast_done[22 + tc])
                sync.dma_start(out=wld(p), in_=wf_h.ap()[sl, :]).then_inc(sem_wl[p], 16)
            emit("sync", ldf)
            St.wl[p] += 16
            need_wl = St.wl[p]

            def red(v, tc=tc, p=p, need_wl=need_wl):
                v.wait_ge(sem_wl[p], need_wl)
                if tc == 0:
                    v.reduce_max(amax_acc[:], wld(0), axis=AX.X,
                                 apply_absolute_value=True).then_inc(sem_v, 1)
                else:
                    v.reduce_max(t1_sb[:], wld(p), axis=AX.X,
                                 apply_absolute_value=True).then_inc(sem_v, 1)
                    v.tensor_max(amax_acc[:], amax_acc[:], t1_sb[:]).then_inc(sem_v, 1)
            emit("vector", red)
            St.v += 1 if tc == 0 else 2
            wamax_done[tc] = St.v
        wamax_v = St.v

        def wam_x(g, need=wamax_v):
            g.wait_ge(sem_v, need)
            g.dma_start(out=row_sc[:], in_=amax_acc[:, 0:1]).then_inc(sem_sg, 16)
        emit("gpsimd", wam_x)
        St.sg += 16
        need_sg = St.sg

        def wam_r(v, need=need_sg):
            v.wait_ge(sem_sg, need)
            v.reduce_max(row_sc[0:1, 0:1], row_sc[:], axis=AX.X).then_inc(sem_v, 1)
        emit("vector", wam_r)
        St.v += 1
        need_v = St.v

        def wam_b(g, need=need_v):
            g.wait_ge(sem_v, need)
            g.dma_start(out=bounce.ap()[0:1, 0:1], in_=row_sc[0:1, 0:1]).then_inc(sem_sg, 16)
        emit("gpsimd", wam_b)
        St.sg += 16
        need_sg = St.sg

        def wam_bc(sync, need=need_sg, gate=c_cast2_v):
            sync.wait_ge(sem_v, gate)
            sync.wait_ge(sem_sg, need)
            sync.dma_start(out=swq_sb[:, 0:1],
                           in_=bounce.ap()[0:1, 0:1].partition_broadcast(128)).then_inc(sem_cst, 16)
        emit("sync", wam_bc)
        St.cst += 16
        cst3 = St.cst

        def wscale(v, need=cst3, base=St.v):
            v.wait_ge(sem_cst, need)
            v.tensor_scalar(t1_sb[:], swq_sb[:, 0:1], 1e-12, None, ALU.max).then_inc(sem_v, 1)
            v.tensor_tensor(thrw_sb[:], t1_sb[:], t1_sb[:], ALU.mult).then_inc(sem_v, 1)
            v.tensor_scalar(thrw_sb[:], thrw_sb[:], 4.0 / 49.0, None, ALU.mult).then_inc(sem_v, 1)
            v.wait_ge(sem_v, base + 1)
            v.reciprocal(t2_sb[:], t1_sb[:]).then_inc(sem_v, 1)
            v.tensor_scalar(swq_sb[:], t2_sb[:], 448.0, None, ALU.mult).then_inc(sem_v, 1)
            v.wait_ge(sem_v, base + 4)
            v.tensor_scalar(swqh_sb[:], t2_sb[:], 224.0, None, ALU.mult).then_inc(sem_v, 1)
            v.tensor_scalar(cpw_sb[:], t1_sb[:], 240.0 / 448.0 * 0.99999, None, ALU.mult).then_inc(sem_v, 1)
            v.tensor_scalar(cnw_sb[:], cpw_sb[:], -1.0, None, ALU.mult).then_inc(sem_v, 1)
        emit("vector", wscale)
        St.v += 8
        wscale_v = St.v

        wq_vdone = {}
        wq_tdone = {}
        wq_qp = {}
        wq_msk_v = {}
        for tc in range(DC):
            sl = slice(tc * 128, (tc + 1) * 128)
            p = tc % 2

            def ldf2(sync, sl=sl, tc=tc, p=p):
                if tc >= 2:
                    sync.wait_ge(sem_v, wq_vdone[tc - 2])
                else:
                    sync.wait_ge(sem_v, wamax_done[6 + tc])
                sync.dma_start(out=wld(p), in_=wf_h.ap()[sl, :]).then_inc(sem_wl[p], 16)
            emit("sync", ldf2)
            St.wl[p] += 16
            need_wl = St.wl[p]

            def wclip(v, tc=tc, p=p, need_wl=need_wl, need_v=wscale_v):
                v.wait_ge(sem_wl[p], need_wl)
                if tc == 0:
                    v.wait_ge(sem_v, need_v)
                v.tensor_scalar(accw[:, 0], wld(p), cpw_sb[:, 0:1], cnw_sb[:, 0:1],
                                ALU.min, ALU.max).then_inc(sem_v, 1)
            emit("vector", wclip)
            St.v += 1
            wclip_v = St.v

            def q_s(s, tc=tc, p=p, need_wl=need_wl, need_clip=wclip_v,
                    need_qv=(wq_vdone.get(tc - 1) if tc >= 1 else None)):
                s.wait_ge(sem_wl[p], need_wl)
                s.wait_ge(sem_v, need_clip)
                if need_qv is not None:
                    s.wait_ge(sem_v, need_qv)
                s.activation(lo8_t[:], accw[:, 0], AF.Copy, scale=swq_sb[:, 0:1]).then_inc(sem_s, 1)
                s.activation(hi8_t[:], wld(p), AF.Copy, scale=swqh_sb[:, 0:1]).then_inc(sem_s, 1)
            emit("scalar", q_s)
            St.s += 2
            need_s = St.s

            wq_msk_v[tc] = St.v + 2
            def q_v(v, tc=tc, p=p, need_s=need_s, wpf=wp_final):
                v.wait_ge(sem_s, need_s)
                if tc == 0:
                    v.wait_ge(sem_wp[0], wpf[0])
                    v.wait_ge(sem_wp[1], wpf[1])
                if tc >= 1:
                    v.wait_ge(sem_t, wq_tdone[tc - 1])
                v.scalar_tensor_tensor(hi2_t[:], wld(p), 1.0, wld(p),
                                       ALU.bypass, ALU.mult).then_inc(sem_v, 1)
                v.tensor_scalar(msk8[:], hi2_t[:], thrw_sb[:, 0:1], None,
                                ALU.is_gt).then_inc(sem_v, 1)
                v.tensor_scalar(hi2_t[:], hi8_t[:], 2.0, None, ALU.mult).then_inc(sem_v, 1)
                v.wait_ge(sem_v, wq_msk_v[tc])
                v.select(q_bf(), msk8[:], hi2_t[:], lo8_t[:], add_drain=True).then_inc(sem_v, 1)
            emit("vector", q_v)
            St.v += 4
            wq_vdone[tc] = St.v
            need_v = St.v

            def q_t(t, need_v=need_v, need_pstr=St.pstr_evac[0]):
                t.wait_ge(sem_v, need_v)
                if need_pstr:
                    t.wait_ge(sem_v, need_pstr)
                ins = None
                for blk in range(8):
                    ins = t.transpose(ps_tr[:, blk * 128:(blk + 1) * 128],
                                      q_bf()[:, blk * 128:(blk + 1) * 128], ident_sb[:])
                ins.then_inc(sem_t, 1)
            emit("tensor", q_t)
            St.t += 1
            wq_tdone[tc] = St.t
            need_t = St.t

            def q_e(v, tc=tc, need_t=need_t, need_qp=St.qp):
                v.wait_ge(sem_t, need_t)
                if need_qp:
                    v.wait_ge(sem_qp, need_qp)
                v.tensor_copy(xqT_st[:], ps_tr[:]).then_inc(sem_v, 1)
            emit("vector", q_e)
            St.v += 1
            St.pstr_evac[0] = St.v
            need_v = St.v

            def q_p(g, tc=tc, need_v=need_v):
                g.wait_ge(sem_v, need_v)
                for blk in range(8):
                    g.dma_start(out=w8T_d.ap()[blk * 128:(blk + 1) * 128,
                                               tc * 128:(tc + 1) * 128],
                                in_=xqT_st[:, blk * 128:(blk + 1) * 128]).then_inc(sem_qp, 16)
            emit("gpsimd", q_p)
            St.qp += 8 * 16
            wq_qp[tc] = St.qp
        qp_final = St.qp
        w2_end_t = St.t

        ph0_vdone = {}
        ph0_sdone = {}
        ph0_gate_s = St.s
        for gt in range(256):
            b, tt = divmod(gt, NT)
            rs = slice(tt * 128, (tt + 1) * 128)
            c = slice(gt, gt + 1)
            slot = gt % 4
            buf = xw[:, slot] if slot < 2 else accw[:, slot - 2]

            def ld0(sync, b=b, rs=rs, gt=gt, slot=slot, buf=buf):
                if gt >= 4:
                    sync.wait_ge(sem_v, ph0_vdone[gt - 4])
                    sync.wait_ge(sem_s, ph0_sdone[gt - 4])
                elif gt < 2:
                    sync.wait_ge(sem_v, wq_vdone[6 + gt])
                elif gt == 2:
                    sync.wait_ge(sem_s, ph0_gate_s)
                sync.dma_start(out=buf, in_=xf3[b, rs, :]).then_inc(
                    (sem_x[0], sem_x[1], sem_f1, sem_f2)[slot], 16)
            emit("sync", ld0)
            if slot == 0:
                St.x[0] += 16
                xsem, need_x = sem_x[0], St.x[0]
            elif slot == 1:
                St.x[1] += 16
                xsem, need_x = sem_x[1], St.x[1]
            elif slot == 2:
                St.f1 += 16
                xsem, need_x = sem_f1, St.f1
            else:
                St.f2 += 16
                xsem, need_x = sem_f2, St.f2

            def sq0(s, gt=gt, c=c, buf=buf, xsem=xsem, need_x=need_x):
                s.wait_ge(xsem, need_x)
                if gt == 0:
                    s.wait_ge(sem_v, c_cast2_v)
                s.activation(sqw(), buf, AF.Identity,
                             accum_out=fsum[:, c]).then_inc(sem_s, 1)
                s.activation(sqw(), buf, AF.Square,
                             accum_out=fsq[:, c]).then_inc(sem_s, 1)
            emit("scalar", sq0)
            St.s += 2
            ph0_sdone[gt] = St.s

            def st0(v, gt=gt, c=c, buf=buf, xsem=xsem, need_x=need_x):
                v.wait_ge(xsem, need_x)
                v.reduce_max(fmax[:, c], buf, axis=AX.X).then_inc(sem_v, 1)
                v.tensor_reduce(fmin[:, c], buf, AX.X, ALU.min).then_inc(sem_v, 1)
            emit("vector", st0)
            St.v += 2
            ph0_vdone[gt] = St.v
        ph0_all_s = St.s

        slA = slice(0, 256)

        def comb1(v, need_s=ph0_all_s, base=St.v):
            v.wait_ge(sem_s, need_s)
            k = [base]

            def w(ins):
                k[0] += 1
                ins.then_inc(sem_v, 1)
                v.wait_ge(sem_v, k[0])
            w(v.tensor_scalar(tmp_f[:, slA], fsum[:, slA], -1.0 / D, None, ALU.mult))
            w(v.tensor_scalar(fsq[:, slA], fsq[:, slA], 1.0 / D, None, ALU.mult))
            w(v.scalar_tensor_tensor(fsum[:, slA], tmp_f[:, slA], 1.0, tmp_f[:, slA],
                                     ALU.bypass, ALU.mult))
            w(v.tensor_sub(fsq[:, slA], fsq[:, slA], fsum[:, slA]))
            w(v.tensor_scalar(fsq[:, slA], fsq[:, slA], 1e-5, None, ALU.add))
            w(v.reciprocal(fsum[:, slA], fsq[:, slA]))
        emit("vector", comb1)
        St.v += 6
        need_v = St.v

        def comb_sqrt(s, need=need_v):
            s.wait_ge(sem_v, need)
            s.activation(fsum[:, slA], fsum[:, slA], AF.Sqrt).then_inc(sem_s, 1)
        emit("scalar", comb_sqrt)
        St.s += 1
        need_s = St.s

        def comb2(v, need=need_s, base=St.v):
            v.wait_ge(sem_s, need)
            k = [base]

            def w(ins):
                k[0] += 1
                ins.then_inc(sem_v, 1)
                v.wait_ge(sem_v, k[0])
            for _ in range(2):
                w(v.scalar_tensor_tensor(tmp_g[:, slA], fsum[:, slA], 1.0, fsum[:, slA],
                                         ALU.bypass, ALU.mult))
                w(v.tensor_tensor(tmp_g[:, slA], tmp_g[:, slA], fsq[:, slA], ALU.mult))
                w(v.tensor_scalar(tmp_g[:, slA], tmp_g[:, slA], -0.5, 1.5, ALU.mult, ALU.add))
                w(v.tensor_tensor(fsum[:, slA], fsum[:, slA], tmp_g[:, slA], ALU.mult))
            w(v.tensor_tensor(fmax[:, slA], fmax[:, slA], tmp_f[:, slA], ALU.add))
            w(v.tensor_tensor(fmin[:, slA], fmin[:, slA], tmp_f[:, slA], ALU.add))
            w(v.tensor_scalar(fmin[:, slA], fmin[:, slA], -1.0, None, ALU.mult))
            w(v.tensor_max(fmax[:, slA], fmax[:, slA], fmin[:, slA]))
            w(v.tensor_tensor(fmax[:, slA], fmax[:, slA], fsum[:, slA], ALU.mult))
            w(v.reduce_max(amax_acc[:], fmax[:, slA], axis=AX.X))
            w(v.tensor_scalar(tmp_g[:, slA], fsum[:, slA], 1.0, None, ALU.add))
            w(v.tensor_tensor(fmin[:, slA], tmp_f[:, slA], fsum[:, slA], ALU.mult))
        emit("vector", comb2)
        St.v += 16
        amax_v = St.v

        cid = nc.partition_id([mybir.EngineType.Pool])

        def own_cp(g, need=amax_v):
            g.wait_ge(sem_v, need)
            g.dma_start(out=mneg_st[:, 0:2 * NT], in_=tmp_f[:, bass.ds(cid * 32, 32)]).then_inc(sem_sg, 16)
            g.dma_start(out=r_st[:, 0:2 * NT], in_=fsum[:, bass.ds(cid * 32, 32)]).then_inc(sem_sg, 16)
            g.dma_start(out=s1p_st[:, 0:2 * NT], in_=tmp_g[:, bass.ds(cid * 32, 32)]).then_inc(sem_sg, 16)
            g.dma_start(out=s2p_st[:, 0:2 * NT], in_=fmin[:, bass.ds(cid * 32, 32)]).then_inc(sem_sg, 16)
        emit("gpsimd", own_cp)
        St.sg += 64

        def am_x(g, need=amax_v):
            g.wait_ge(sem_v, need)
            g.dma_start(out=row_sc[:], in_=amax_acc[:, 0:1]).then_inc(sem_sg, 16)
        emit("gpsimd", am_x)
        St.sg += 16
        own_sg = St.sg
        need_sg = St.sg

        def am_r(v, need=need_sg):
            v.wait_ge(sem_sg, need)
            v.reduce_max(row_sc[0:1, 0:1], row_sc[:], axis=AX.X).then_inc(sem_v, 1)
        emit("vector", am_r)
        St.v += 1
        need_v = St.v

        def am_b(g, need=need_v):
            g.wait_ge(sem_v, need)
            g.dma_start(out=bounce.ap()[0:1, 1:2], in_=row_sc[0:1, 0:1]).then_inc(sem_sg, 16)
        emit("gpsimd", am_b)
        St.sg += 16
        need_sg = St.sg

        def am_bc(sync, need=need_sg, gate=wscale_v):
            sync.wait_ge(sem_v, gate)
            sync.wait_ge(sem_sg, need)
            sync.dma_start(out=amax_bc[:, 0:1],
                           in_=bounce.ap()[0:1, 1:2].partition_broadcast(128)).then_inc(sem_cst, 16)
        emit("sync", am_bc)
        St.cst += 16
        cst4 = St.cst

        def gscale(v, need=cst4, own=own_sg, base=St.v):
            v.wait_ge(sem_cst, need)
            v.wait_ge(sem_sg, own)
            k = [base]

            def w(ins):
                k[0] += 1
                ins.then_inc(sem_v, 1)
                v.wait_ge(sem_v, k[0])
            w(v.tensor_scalar(t1_sb[:], amax_bc[:, 0:1], 1e-12, None, ALU.max))
            w(v.tensor_tensor(thr_sb[:], t1_sb[:], t1_sb[:], ALU.mult))
            w(v.tensor_scalar(thr_sb[:], thr_sb[:], 4.0 / 49.0, None, ALU.mult))
            w(v.reciprocal(t2_sb[:], t1_sb[:]))
            w(v.tensor_scalar(sx_sb[:], t2_sb[:], 448.0, None, ALU.mult))
            w(v.tensor_scalar(sxh_sb[:], t2_sb[:], 224.0, None, ALU.mult))
            w(v.tensor_scalar(cpx_sb[:], t1_sb[:], 240.0 / 448.0 * 0.99999, None, ALU.mult))
            w(v.tensor_scalar(cnx_sb[:], cpx_sb[:], -1.0, None, ALU.mult))
            w(v.tensor_tensor(sprod_sb[:], sx_sb[:], swq_sb[:], ALU.mult))
            w(v.reciprocal(scinv_sb[:], sprod_sb[:]))
            w(v.tensor_scalar(bf_rs[:], bf_row[:], sprod_sb[0:1, 0:1], None, ALU.mult))
        emit("vector", gscale)
        St.v += 11
        gscale_v = St.v

        last_wbf_reader_t = 0
        last_resc_v = 0
        last_pv_t = 0
        acc_parked = {}
        out_done = {}
        for b in range(B_LOC):
            for tc in range(DC):
                def ldw8(sync, tc=tc, b=b, need_qp=qp_final,
                         need_t=last_wbf_reader_t, need_v=last_resc_v):
                    if tc == 0:
                        if b == 0:
                            sync.wait_ge(sem_qp, need_qp)
                        if need_t:
                            sync.wait_ge(sem_t, need_t)
                        if need_v:
                            sync.wait_ge(sem_v, need_v)
                    sync.dma_start(out=wbf(tc),
                                   in_=w8T_d.ap()[tc * 128:(tc + 1) * 128, :]).then_inc(sem_w, 16)
                emit("sync", ldw8)
                St.w += 16
            w8T_in = St.w

            tr2_done = {}
            for tt in range(NT):
                gt = b * NT + tt
                rs = slice(tt * 128, (tt + 1) * 128)
                c = slice(gt, gt + 1)
                p = tt % 2

                px = tt % 3

                def ld1(sync, b=b, rs=rs, tt=tt, px=px,
                        gate_pk=((tt - 3) % 2 if tt >= 3 else (NT - 3 + tt) % 2,
                                 acc_parked.get((b, tt - 3)) if tt >= 3 else
                                 acc_parked.get((b - 1, NT - 3 + tt))),
                        gate_v=(ph0_vdone[252 + tt] if (b == 0 and tt < 2) else None),
                        gate_s=(ph0_sdone[252 + tt] if (b == 0 and tt < 2) else None)):
                    if gate_pk[1] is not None:
                        sync.wait_ge(sem_pk[gate_pk[0]], gate_pk[1])
                    if gate_v is not None:
                        sync.wait_ge(sem_v, gate_v)
                    if gate_s is not None:
                        sync.wait_ge(sem_s, gate_s)
                    sync.dma_start(out=xw[:, px], in_=x3[b, rs, :]).then_inc(sem_x[px], 16)
                emit("sync", ld1)
                St.x[px] += 16
                need_x = St.x[px]

                def xnS(s, c=c, px=px, need_x=need_x,
                        first=(b == 0 and tt == 0), gv=gscale_v):
                    s.wait_ge(sem_x[px], need_x)
                    if first:
                        s.wait_ge(sem_v, gv)
                    s.activation(xn_f[:], xw[:, px], AF.Identity,
                                 scale=r_st[:, c], bias=s2p_st[:, c]).then_inc(sem_s, 1)
                emit("scalar", xnS)
                St.s += 1
                need_xs = St.s

                def xn1(v, p=p, need_xs=need_xs,
                        need_pk=(acc_parked.get((b, tt - 2))),
                        need_y=(out_done.get((b - 1, NT - 2 + tt))
                                if (b >= 1 and tt < 2) else None)):
                    v.wait_ge(sem_s, need_xs)
                    if need_pk is not None:
                        v.wait_ge(sem_pk[p], need_pk)
                    if need_y is not None:
                        v.wait_ge(sem_y[p], need_y)
                    v.scalar_tensor_tensor(hi2_t[:], xn_f[:], 1.0, xn_f[:],
                                           ALU.bypass, ALU.mult).then_inc(sem_v, 1)
                    v.tensor_scalar(msk8[:], hi2_t[:], thr_sb[:, 0:1], None,
                                    ALU.is_gt).then_inc(sem_v, 1)
                    v.tensor_scalar(accw[:, p], xn_f[:], cpx_sb[:, 0:1], cnx_sb[:, 0:1],
                                    ALU.min, ALU.max).then_inc(sem_v, 1)
                emit("vector", xn1)
                St.v += 3
                need_v = St.v

                def q1s(s, tt=tt, b=b, c=c, p=p, px=px, need_v=need_v,
                        need_t=(tr2_done.get(tt - 1) if tt >= 1 else
                                (w2_end_t if b == 0 else last_pv_t)),
                        need_pk=(acc_parked.get((b, tt - 2))),
                        need_y=(out_done.get((b - 1, NT - 2 + tt))
                                if (b >= 1 and tt < 2) else None)):
                    s.wait_ge(sem_v, need_v)
                    if need_t:
                        s.wait_ge(sem_t, need_t)
                    if need_pk is not None:
                        s.wait_ge(sem_pk[p], need_pk)
                    if need_y is not None:
                        s.wait_ge(sem_y[p], need_y)
                    s.activation(lo8_t[:], accw[:, p], AF.Copy, scale=sx_sb[:, 0:1]).then_inc(sem_s, 1)
                    s.activation(hi8_t[:], xn_f[:], AF.Copy, scale=sxh_sb[:, 0:1]).then_inc(sem_s, 1)
                    s.activation(xn_bf(), xn_f[:], AF.Copy).then_inc(sem_s, 1)
                    s.activation(accw[:, p], xw[:, px], AF.Identity,
                                 scale=s1p_st[:, c], bias=s2p_st[:, c]).then_inc(sem_s, 1)
                emit("scalar", q1s)
                St.s += 4
                acc0_s = St.s
                need_s = St.s

                def q1v(v, need_s=need_s,
                        need_t=(tr2_done.get(tt - 1) if tt >= 1 else
                                (w2_end_t if b == 0 else last_pv_t))):
                    v.wait_ge(sem_s, need_s)
                    if need_t:
                        v.wait_ge(sem_t, need_t)
                    v.tensor_scalar(hi2_t[:], hi8_t[:], 2.0, None, ALU.mult).then_inc(sem_v, 1)
                    v.select(q_bf(), msk8[:], hi2_t[:], lo8_t[:], add_drain=True).then_inc(sem_v, 1)
                emit("vector", q1v)
                St.v += 2
                need_v = St.v

                def tr1(t, need_v=need_v, need_pstr=St.pstr_evac[0]):
                    t.wait_ge(sem_v, need_v)
                    t.wait_ge(sem_v, need_pstr)
                    ins = None
                    for blk in range(8):
                        ins = t.transpose(ps_tr[:, blk * 128:(blk + 1) * 128],
                                          xn_bf()[:, blk * 128:(blk + 1) * 128], ident_sb[:])
                    ins.then_inc(sem_t, 1)
                emit("tensor", tr1)
                St.t += 1
                need_t = St.t

                def ev1(v, rs=rs, need_t=need_t):
                    v.wait_ge(sem_t, need_t)
                    for blk in range(8):
                        v.tensor_copy(xnT(blk)[:, rs],
                                      ps_tr[:, blk * 128:(blk + 1) * 128]).then_inc(sem_v, 1)
                emit("vector", ev1)
                St.v += 8
                St.pstr_evac[0] = St.v

                def tr2(t, need_v=need_v, need_pstr=St.pstr_evac[1]):
                    t.wait_ge(sem_v, need_v)
                    if need_pstr:
                        t.wait_ge(sem_v, need_pstr)
                    ins = None
                    for blk in range(8):
                        ins = t.transpose(ps_tr2[:, blk * 128:(blk + 1) * 128],
                                          q_bf()[:, blk * 128:(blk + 1) * 128], ident_sb[:])
                    ins.then_inc(sem_t, 1)
                emit("tensor", tr2)
                St.t += 1
                tr2_done[tt] = St.t
                need_t = St.t

                def ev2(v, b=b, tt=tt, need_t=need_t, need_qp=qp_final):
                    v.wait_ge(sem_t, need_t)
                    if b == 0 and tt == 0:
                        v.wait_ge(sem_qp, need_qp)
                    v.tensor_copy(xqT_st[:], ps_tr2[:]).then_inc(sem_v, 1)
                emit("vector", ev2)
                St.v += 1
                St.pstr_evac[1] = St.v
                xq_ready_v = St.v

                for oc, bank in ((0, BANKS[(2 * tt + 0) % 4]), (1, BANKS[(2 * tt + 1) % 4])):
                    osl = slice(oc * 512, (oc + 1) * 512)

                    def ffn_mm(t, oc=oc, osl=osl, bank=bank, tt=tt,
                               need_v=xq_ready_v, w_in=w8T_in,
                               bank_free=St.bank_evac[bank]):
                        t.wait_ge(sem_v, need_v)
                        if tt == 0 and oc == 0:
                            t.wait_ge(sem_w, w_in)
                        if bank_free:
                            t.wait_ge(sem_pvv if bank_free[0] == "v" else sem_pvs,
                                      bank_free[1])
                        for dcc in range(DC):
                            t.matmul(PS[bank][:, :],
                                     xqT_st[:, dcc * 128:(dcc + 1) * 128],
                                     wbf(dcc)[:, osl],
                                     start=(dcc == 0), stop=False)
                        t.matmul(PS[bank][:, :], ones_sb[0:1, :], bf_rs[0:1, osl],
                                 start=False, stop=True).then_inc(sem_t, 1)
                    emit("tensor", ffn_mm)
                    St.t += 1
                    last_wbf_reader_t = St.t
                    need_t = St.t

                    def ffn_ev(v, osl=osl, bank=bank, p=p,
                               need_t=need_t, need_s=acc0_s):
                        v.wait_ge(sem_t, need_t)
                        v.wait_ge(sem_s, need_s)
                        v.scalar_tensor_tensor(accw[:, p, osl.start:osl.stop],
                                               PS[bank][:, :], scinv_sb[:, 0:1],
                                               accw[:, p, osl.start:osl.stop],
                                               ALU.mult, ALU.add).then_inc(sem_v, 1)
                        v.engine_nop().then_inc(sem_pvv, 1)
                    emit("vector", ffn_ev)
                    St.v += 1
                    St.pv_v += 1
                    St.bank_evac[bank] = ("v", St.pv_v)

                def park1(g, b=b, rs=rs, p=p, need_v=St.v):
                    g.wait_ge(sem_v, need_v)
                    g.dma_start(out=acc3[b, rs, :], in_=accw[:, p]).then_inc(sem_pk[p], 16)
                emit("gpsimd", park1)
                St.pk[p] += 16
                acc_parked[(b, tt)] = St.pk[p]

            xnT_all_v = St.v

            for name, wi, biascol in (("q", 0, bq_sb), ("k", 1, bk_sb), ("v", 2, None)):
                for tc in range(DC):
                    def ldq(sync, wi=wi, tc=tc, name=name, b=b,
                            need_t=last_wbf_reader_t, wpf=wp_final):
                        if tc == 0:
                            sync.wait_ge(sem_t, need_t)
                            if name == "q" and b == 0:
                                sync.wait_ge(sem_wp[0], wpf[0])
                                sync.wait_ge(sem_wp[1], wpf[1])
                        sync.dma_start(out=wbf(tc),
                                       in_=wqkv_bf.ap()[wi, tc * 128:(tc + 1) * 128, :]).then_inc(sem_w, 16)
                    emit("sync", ldq)
                    St.w += 16
                w_in = St.w

                if name in ("q", "k"):
                    dst = qt if name == "q" else kt
                    for kc in range(DC):
                        for ich in range(NIC):
                            isl = slice(ich * 512, (ich + 1) * 512)
                            bank = BANKS[(kc * NIC + ich) % 4]

                            def qk_mm(t, kc=kc, isl=isl, bank=bank, w_in=w_in,
                                      first=(kc == 0 and ich == 0),
                                      need_v=xnT_all_v,
                                      bank_free=St.bank_evac[bank]):
                                if first:
                                    t.wait_ge(sem_w, w_in)
                                    t.wait_ge(sem_v, need_v)
                                if bank_free:
                                    t.wait_ge(sem_pvv if bank_free[0] == "v" else sem_pvs,
                                              bank_free[1])
                                ins = None
                                for dcc in range(DC):
                                    ins = t.matmul(PS[bank][:, :],
                                                   wbf(dcc)[:, kc * 128:(kc + 1) * 128],
                                                   xnT(dcc)[:, isl],
                                                   start=(dcc == 0), stop=(dcc == DC - 1))
                                ins.then_inc(sem_t, 1)
                            emit("tensor", qk_mm)
                            St.t += 1
                            last_wbf_reader_t = St.t
                            need_t = St.t

                            def qk_ev(s, kc=kc, isl=isl, bank=bank, dst=dst,
                                      biascol=biascol, need_t=need_t):
                                s.wait_ge(sem_t, need_t)
                                s.activation(dst(kc)[:, isl], PS[bank][:, :], AF.Identity,
                                             bias=biascol[:, kc:kc + 1]).then_inc(sem_s, 1)
                                s.nop().then_inc(sem_pvs, 1)
                            emit("scalar", qk_ev)
                            St.s += 1
                            St.pv_s += 1
                            St.bank_evac[bank] = ("s", St.pv_s)
                else:
                    for jt in range(NT):
                        js = slice(jt * 128, (jt + 1) * 128)
                        for vc in range(2):
                            vsl = slice(vc * 512, (vc + 1) * 512)
                            bank = BANKS[(jt * 2 + vc) % 4]

                            def v_mm(t, js=js, vsl=vsl, bank=bank, w_in=w_in,
                                     first=(jt == 0 and vc == 0),
                                     bank_free=St.bank_evac[bank]):
                                if first:
                                    t.wait_ge(sem_w, w_in)
                                if bank_free:
                                    t.wait_ge(sem_pvv if bank_free[0] == "v" else sem_pvs,
                                              bank_free[1])
                                for dcc in range(DC):
                                    t.matmul(PS[bank][:, :],
                                             xnT(dcc)[:, js],
                                             wbf(dcc)[:, vsl],
                                             start=(dcc == 0), stop=False)
                                t.matmul(PS[bank][:, :], ones_sb[0:1, :], bv_row[0:1, vsl],
                                         start=False, stop=True).then_inc(sem_t, 1)
                            emit("tensor", v_mm)
                            St.t += 1
                            last_wbf_reader_t = St.t
                            need_t = St.t

                            def v_ev(s, jt=jt, vsl=vsl, bank=bank, need_t=need_t):
                                s.wait_ge(sem_t, need_t)
                                s.activation(vt(jt)[:, vsl], PS[bank][:, :], AF.Copy).then_inc(sem_s, 1)
                                s.nop().then_inc(sem_pvs, 1)
                            emit("scalar", v_ev)
                            St.s += 1
                            St.pv_s += 1
                            St.bank_evac[bank] = ("s", St.pv_s)

            for jt in range(NT):
                j0 = jt * 128
                js = slice(j0, j0 + 128)
                vcs = valid_chunks(jt)
                for n_i, ic in enumerate(vcs):
                    isl = slice(ic * 512, (ic + 1) * 512)
                    bank = BANKS[sum(len(valid_chunks(j)) for j in range(jt)) % 4 if False else (jt * NIC + ic) % 4]

                    def s_mm(t, js=js, isl=isl, bank=bank,
                             bank_free=St.bank_evac[bank]):
                        if bank_free:
                            t.wait_ge(sem_pvv if bank_free[0] == "v" else sem_pvs,
                                      bank_free[1])
                        ins = None
                        for kc in range(DC):
                            ins = t.matmul(PS[bank][:, :],
                                           kt(kc)[:, js],
                                           qt(kc)[:, isl],
                                           start=(kc == 0), stop=(kc == DC - 1))
                        ins.then_inc(sem_t, 1)
                    emit("tensor", s_mm)
                    St.t += 1
                    need_t = St.t

                    delta = j0 - ic * 512
                    partial = delta > -127

                    mrun_v = St.v + (3 if partial else 2)
                    def s_st(v, n_i=n_i, bank=bank, partial=partial, delta=delta,
                             need_t=need_t, mrun_v=mrun_v):
                        v.wait_ge(sem_t, need_t)
                        if partial:
                            off = 512 - delta
                            v.tensor_add(PS[bank][:, :], PS[bank][:, :],
                                         band_sb[:, off:off + 512]).then_inc(sem_v, 1)
                        v.reduce_max(tinyA[:, n_i:n_i + 1], PS[bank][:, :], axis=AX.X).then_inc(sem_v, 1)
                        if n_i == 0:
                            v.tensor_copy(mrun[:], tinyA[:, 0:1]).then_inc(sem_v, 1)
                        else:
                            v.tensor_max(mrun[:], mrun[:], tinyA[:, n_i:n_i + 1]).then_inc(sem_v, 1)
                        v.tensor_copy(msave[:, n_i:n_i + 1], mrun[:]).then_inc(sem_v, 1)
                        v.wait_ge(sem_v, mrun_v)
                        v.tensor_scalar(mnegs[:, n_i:n_i + 1], mrun[:], -1.0, None,
                                        ALU.mult).then_inc(sem_v, 1)
                    emit("vector", s_st)
                    St.v += 5 if partial else 4
                    need_v = St.v

                    def s_exp(s, jt=jt, ic=ic, bank=bank, n_i=n_i, need_v=need_v):
                        s.wait_ge(sem_v, need_v)
                        s.activation(e_chunk(jt, ic), PS[bank][:, :], AF.Exp,
                                     bias=mnegs[:, n_i:n_i + 1],
                                     accum_out=ssave[:, n_i:n_i + 1]).then_inc(sem_s, 1)
                        s.nop().then_inc(sem_pvs, 1)
                    emit("scalar", s_exp)
                    St.s += 1
                    St.pv_s += 1
                    St.bank_evac[bank] = ("s", St.pv_s)
                exp_done_s = St.s
                nv = len(vcs)

                def fin_v(v, nv=nv, need_s=exp_done_s, base=St.v):
                    v.wait_ge(sem_s, need_s)
                    k = [base]

                    def w(ins):
                        k[0] += 1
                        ins.then_inc(sem_v, 1)
                        v.wait_ge(sem_v, k[0])
                    w(v.reduce_max(mrun[:], tinyA[:, 0:nv], axis=AX.X))
                    w(v.tensor_scalar(tinyA[:, 0:nv], tinyA[:, 0:nv], mrun[:, 0:1], None,
                                      ALU.subtract))
                emit("vector", fin_v)
                St.v += 2
                need_v = St.v

                def fin_s(s, nv=nv, need_v=need_v):
                    s.wait_ge(sem_v, need_v)
                    s.activation(tinyB[:, 0:nv], tinyA[:, 0:nv], AF.Exp).then_inc(sem_s, 1)
                emit("scalar", fin_s)
                St.s += 1
                need_s = St.s

                def fin_v2(v, jt=jt, nv=nv, need_s=need_s):
                    v.wait_ge(sem_s, need_s)
                    v.tensor_tensor(tinyA[:, 0:nv], tinyB[:, 0:nv], ssave[:, 0:nv], ALU.mult).then_inc(sem_v, 1)
                    v.reduce_sum(lsum[:], tinyA[:, 0:nv], axis=AX.X).then_inc(sem_v, 1)
                    v.reciprocal(lsum[:], lsum[:]).then_inc(sem_v, 1)
                    v.tensor_scalar(lsum[:], lsum[:], 1.0 / 32.0, None, ALU.mult).then_inc(sem_v, 1)
                    v.tensor_scalar(gall[:, jt, 0:nv], tinyB[:, 0:nv], lsum[:, 0:1], None,
                                    ALU.mult).then_inc(sem_v, 1)
                emit("vector", fin_v2)
                St.v += 5

            pv_mm_t = {}
            for it in range(NT):
                i0 = it * 128
                icov = i0 // 512
                p = it % 2

                def acc_ld(g, b=b, rs=slice(i0, i0 + 128), it=it, p=p,
                           need_y=out_done.get((b, it - 2)),
                           need_pk=(acc_parked.get((b, NT - 2 + it)) if it < 2 else None)):
                    if need_y is not None:
                        g.wait_ge(sem_y[p], need_y)
                    if need_pk is not None:
                        g.wait_ge(sem_pk[p], need_pk)
                    g.dma_start(out=accw[:, p], in_=acc3[b, rs, :]).then_inc(sem_acc[p], 16)
                emit("gpsimd", acc_ld)
                St.acc[p] += 16
                acc_in = St.acc[p]

                def resc(v, it=it, i0=i0, icov=icov,
                         need_t=pv_mm_t.get(it - 1) if it >= 1 else None):
                    if need_t is not None:
                        v.wait_ge(sem_t, need_t)
                    for jt in range(it + 1):
                        gcol = icov - valid_chunks(jt)[0]
                        v.tensor_scalar(pstage(jt),
                                        e_cols(jt, i0, 128),
                                        gall[:, jt, gcol:gcol + 1], None,
                                        ALU.mult).then_inc(sem_v, 1)
                emit("vector", resc)
                St.v += it + 1
                last_resc_v = St.v
                need_v = St.v

                for vc, bank in ((0, BANKS[(2 * it + 0) % 4]), (1, BANKS[(2 * it + 1) % 4])):
                    vsl = slice(vc * 512, (vc + 1) * 512)

                    def pv_mm(t, it=it, vsl=vsl, bank=bank, need_v=need_v,
                              bank_free=St.bank_evac[bank]):
                        t.wait_ge(sem_v, need_v)
                        if bank_free:
                            t.wait_ge(sem_pvv if bank_free[0] == "v" else sem_pvs,
                                      bank_free[1])
                        ins = None
                        for jt in range(it + 1):
                            ins = t.matmul(PS[bank][:, :],
                                           pstage(jt),
                                           vt(jt)[:, vsl],
                                           start=(jt == 0), stop=(jt == it))
                        ins.then_inc(sem_t, 1)
                    emit("tensor", pv_mm)
                    St.t += 1
                    need_t_mm = St.t

                    def merge(v, vsl=vsl, bank=bank, p=p, need_t=need_t_mm,
                              need_acc=acc_in):
                        v.wait_ge(sem_t, need_t)
                        v.wait_ge(sem_acc[p], need_acc)
                        v.tensor_add(accw[:, p, vsl.start:vsl.stop],
                                     accw[:, p, vsl.start:vsl.stop],
                                     PS[bank][:, :]).then_inc(sem_v, 1)
                        v.engine_nop().then_inc(sem_pvv, 1)
                    emit("vector", merge)
                    St.v += 1
                    St.pv_v += 1
                    St.bank_evac[bank] = ("v", St.pv_v)
                pv_mm_t[it] = St.t
                last_pv_t = St.t
                merged_v = St.v

                def out_wr(sync, b=b, rs=slice(i0, i0 + 128), p=p, need_v=merged_v):
                    sync.wait_ge(sem_v, need_v)
                    sync.dma_start(out=y3[b, rs, :], in_=accw[:, p]).then_inc(sem_y[p], 16)
                emit("sync", out_wr)
                St.y[p] += 16
                out_done[(b, it)] = St.y[p]

        yf = (St.y[0], St.y[1])

        def fin(g, yf=yf):
            g.wait_ge(sem_y[0], yf[0])
            g.wait_ge(sem_y[1], yf[1])
        emit("gpsimd", fin)

        class _SerIns:
            def __init__(self, ins, proxy):
                self._ins = ins
                self._proxy = proxy

            def then_inc(self, sem, n):
                self._ins.then_inc(sem, n)
                if sem is self._proxy._own:
                    self._proxy._n += n
                return self

        class _SerEng:
            _OPS = frozenset((
                "tensor_scalar", "tensor_tensor", "tensor_add", "tensor_sub",
                "tensor_max", "tensor_copy", "reduce_max", "reduce_sum",
                "tensor_reduce", "reciprocal", "select", "scalar_tensor_tensor",
                "engine_nop", "memset", "memzero", "activation", "nop",
                "tensor_single_scalar", "copy", "add", "mul", "square", "sqrt",
            ))

            def __init__(self, eng, own_sem):
                self._eng = eng
                self._own = own_sem
                self._n = 0

            def wait_ge(self, sem, val):
                return self._eng.wait_ge(sem, val)

            def __getattr__(self, name):
                target = getattr(self._eng, name)
                if name in _SerEng._OPS:
                    def call(*a, _t=target, **k):
                        if self._n:
                            self._eng.wait_ge(self._own, self._n)
                        return _SerIns(_t(*a, **k), self)
                    return call
                return target

        @block.sync
        def _(sync):
            for fn in prog["sync"]:
                fn(sync)

        @block.gpsimd
        def _(g):
            for fn in prog["gpsimd"]:
                fn(g)

        final_v = St.v
        final_s = St.s

        @block.vector
        def _(v):
            vw = _SerEng(v, sem_v)
            for fn in prog["vector"]:
                fn(vw)
            assert vw._n == final_v, f"vector inc count {vw._n} != bookkeeping {final_v}"

        @block.scalar
        def _(s):
            sw = _SerEng(s, sem_s)
            for fn in prog["scalar"]:
                fn(sw)
            assert sw._n == final_s, f"scalar inc count {sw._n} != bookkeeping {final_s}"

        @block.tensor
        def _(t):
            for fn in prog["tensor"]:
                fn(t)

    nc.finalize()
    return nc


_NC_CACHE = None


def kernel(L0_attn_minibatch, L0_attn_Wq, L0_attn_bq, L0_attn_Wk, L0_attn_bk,
           L0_attn_Wv, L0_attn_bv, L0_ffn_weight, L0_ffn_bias):
    global _NC_CACHE
    x = np.ascontiguousarray(L0_attn_minibatch, dtype=np.float32)
    shards = [np.ascontiguousarray(x[2 * c:2 * c + 2]) for c in range(8)]
    common = {
        "wq": np.ascontiguousarray(L0_attn_Wq, dtype=np.float32),
        "wk": np.ascontiguousarray(L0_attn_Wk, dtype=np.float32),
        "wv": np.ascontiguousarray(L0_attn_Wv, dtype=np.float32),
        "wf": np.ascontiguousarray(L0_ffn_weight, dtype=np.float32),
        "bq": np.ascontiguousarray(L0_attn_bq, dtype=np.float32).reshape(1, D),
        "bk": np.ascontiguousarray(L0_attn_bk, dtype=np.float32).reshape(1, D),
        "bv": np.ascontiguousarray(L0_attn_bv, dtype=np.float32).reshape(1, D),
        "bf": np.ascontiguousarray(L0_ffn_bias, dtype=np.float32).reshape(1, D),
    }
    if _NC_CACHE is None:
        _NC_CACHE = build_program()
    nc = _NC_CACHE
    in_maps = [{"x": shards[c], "xf": x, **common} for c in range(8)]
    res = run_bass_kernel_spmd(nc, in_maps, core_ids=list(range(8)))
    return np.concatenate([res.results[c]["y"] for c in range(8)], axis=0)


# version 18
# speedup vs baseline: 1.0086x; 1.0014x over previous
import numpy as np
import concourse.bass as bass
import concourse.mybir as mybir
from concourse.bacc import Bacc
from concourse.bass_utils import run_bass_kernel_spmd

F32 = mybir.dt.float32
BF16 = mybir.dt.bfloat16
FP8 = mybir.dt.float8e4
U8 = mybir.dt.uint8
AF = mybir.ActivationFunctionType
ALU = mybir.AluOpType
AX = mybir.AxisListType

B_LOC = 2
T = 2048
D = 1024
NT = T // 128
DC = D // 128
NIC = T // 512
NEG = -1.0e30
MASK_W = 1152


def valid_chunks(jt):
    j0 = jt * 128
    return [ic for ic in range(NIC) if (ic + 1) * 512 - 1 >= j0]


E_OFF = {}
_off = 0
for _jt in range(NT):
    E_OFF[_jt] = _off
    _off += len(valid_chunks(_jt)) * 512
E_TOTAL = _off


def build_program():
    import ml_dtypes
    nc = Bacc("TRN2", target_bir_lowering=False)

    x_h = nc.dram_tensor("x", [B_LOC, T, D], F32, kind="ExternalInput")
    xf_h = nc.dram_tensor("xf", [16, T, D], F32, kind="ExternalInput")
    wq_h = nc.dram_tensor("wq", [D, D], F32, kind="ExternalInput")
    wk_h = nc.dram_tensor("wk", [D, D], F32, kind="ExternalInput")
    wv_h = nc.dram_tensor("wv", [D, D], F32, kind="ExternalInput")
    wf_h = nc.dram_tensor("wf", [D, D], F32, kind="ExternalInput")
    bq_h = nc.dram_tensor("bq", [1, D], F32, kind="ExternalInput")
    bk_h = nc.dram_tensor("bk", [1, D], F32, kind="ExternalInput")
    bv_h = nc.dram_tensor("bv", [1, D], F32, kind="ExternalInput")
    bf_h = nc.dram_tensor("bf", [1, D], F32, kind="ExternalInput")
    y_h = nc.dram_tensor("y", [B_LOC, T, D], F32, kind="ExternalOutput")

    wqkv_bf = nc.dram_tensor("wqkv_bf", [3, D, D], BF16, kind="Internal")
    w8T_d = nc.dram_tensor("w8T_d", [D, D], BF16, kind="Internal")
    acc_d = nc.dram_tensor("acc_d", [B_LOC, T, D], F32, kind="Internal")
    bounce = nc.dram_tensor("bounce", [1, 16], F32, kind="Internal")

    ident_h = nc.inline_tensor(np.eye(128, dtype=ml_dtypes.bfloat16), name="ident")
    uu = np.arange(MASK_W)[None, :]
    pp = np.arange(128)[:, None]
    band_np = np.where(uu - pp >= 512, 0.0, NEG).astype(ml_dtypes.bfloat16)
    band_h = nc.inline_tensor(band_np, name="band")
    ones_h = nc.inline_tensor(np.ones((1, 128), dtype=ml_dtypes.bfloat16), name="ones")

    x3 = x_h.ap()
    xf3 = xf_h.ap()
    y3 = y_h.ap()
    acc3 = acc_d.ap()

    from contextlib import ExitStack
    with ExitStack() as _ctx:
        E = _ctx.enter_context
        ident_sb = E(nc.sbuf_tensor("ident_sb", [128, 128], BF16))
        band_sb = E(nc.sbuf_tensor("band_sb", [128, MASK_W], BF16))
        ones_sb = E(nc.sbuf_tensor("ones_sb", [1, 128], BF16))
        bq_sb = E(nc.sbuf_tensor("bq_sb", [128, DC], F32))
        bk_sb = E(nc.sbuf_tensor("bk_sb", [128, DC], F32))
        bv_row = E(nc.sbuf_tensor("bv_row", [1, D], BF16))
        bf_row = E(nc.sbuf_tensor("bf_row", [1, D], BF16))
        mneg_st = E(nc.sbuf_tensor("mneg_st", [128, 2 * NT], F32))
        r_st = E(nc.sbuf_tensor("r_st", [128, 2 * NT], F32))
        s1p_st = E(nc.sbuf_tensor("s1p_st", [128, 2 * NT], F32))
        s2p_st = E(nc.sbuf_tensor("s2p_st", [128, 2 * NT], F32))
        mx_st = E(nc.sbuf_tensor("mx_st", [128, 2 * NT], F32))
        mn_st = E(nc.sbuf_tensor("mn_st", [128, 2 * NT], F32))
        tmp_c = E(nc.sbuf_tensor("tmp_c", [128, 2 * NT], F32))
        amax_acc = E(nc.sbuf_tensor("amax_acc", [128, 1], F32))
        amax_bc = E(nc.sbuf_tensor("amax_bc", [128, 1], F32))
        fmax = E(nc.sbuf_tensor("fmax", [128, 256], F32))
        fmin = E(nc.sbuf_tensor("fmin", [128, 256], F32))
        fsum = E(nc.sbuf_tensor("fsum", [128, 256], F32))
        fsq = E(nc.sbuf_tensor("fsq", [128, 256], F32))
        tmp_f = E(nc.sbuf_tensor("tmp_f", [128, 256], F32))
        tmp_g = E(nc.sbuf_tensor("tmp_g", [128, 256], F32))
        sx_sb = E(nc.sbuf_tensor("sx_sb", [128, 1], F32))
        sxh_sb = E(nc.sbuf_tensor("sxh_sb", [128, 1], F32))
        thr_sb = E(nc.sbuf_tensor("thr_sb", [128, 1], F32))
        row_sc = E(nc.sbuf_tensor("row_sc", [1, 128], F32))
        swq_sb = E(nc.sbuf_tensor("swq_sb", [128, 1], F32))
        swqh_sb = E(nc.sbuf_tensor("swqh_sb", [128, 1], F32))
        thrw_sb = E(nc.sbuf_tensor("thrw_sb", [128, 1], F32))
        t1_sb = E(nc.sbuf_tensor("t1_sb", [128, 1], F32))
        t2_sb = E(nc.sbuf_tensor("t2_sb", [128, 1], F32))
        cpx_sb = E(nc.sbuf_tensor("cpx_sb", [128, 1], F32))
        cnx_sb = E(nc.sbuf_tensor("cnx_sb", [128, 1], F32))
        cpw_sb = E(nc.sbuf_tensor("cpw_sb", [128, 1], F32))
        cnw_sb = E(nc.sbuf_tensor("cnw_sb", [128, 1], F32))
        sprod_sb = E(nc.sbuf_tensor("sprod_sb", [128, 1], F32))
        scinv_sb = E(nc.sbuf_tensor("scinv_sb", [128, 1], F32))
        bf_rs = E(nc.sbuf_tensor("bf_rs", [1, D], BF16))
        xw = E(nc.sbuf_tensor("xw", [128, 3, D], F32))
        xn_f = E(nc.sbuf_tensor("xn_f", [128, D], F32))
        accw = E(nc.sbuf_tensor("accw", [128, 2, D], F32))
        msk8 = E(nc.sbuf_tensor("msk8", [128, D], U8))
        lo8_t = E(nc.sbuf_tensor("lo8", [128, D], FP8))
        hi8_t = E(nc.sbuf_tensor("hi8", [128, D], FP8))
        hi2_t = E(nc.sbuf_tensor("hi2", [128, D], BF16))
        xqT_st = E(nc.sbuf_tensor("xqT_st", [128, D], BF16))
        bscr = E(nc.sbuf_tensor("bscr", [128, T], BF16))
        ereg = E(nc.sbuf_tensor("ereg", [128, 24576], BF16))
        qt_r = E(nc.sbuf_tensor("qt_r", [128, DC, T], BF16))
        kt_r = E(nc.sbuf_tensor("kt_r", [128, DC, T], BF16))
        v_r = E(nc.sbuf_tensor("v_r", [128, NT, D], BF16))
        mrun = E(nc.sbuf_tensor("mrun", [128, 1], F32))
        mnegs = E(nc.sbuf_tensor("mnegs", [128, NIC], F32))
        msave = E(nc.sbuf_tensor("msave", [128, NIC], F32))
        ssave = E(nc.sbuf_tensor("ssave", [128, NIC], F32))
        gall = E(nc.sbuf_tensor("gall", [128, NT, NIC], F32))
        tinyA = E(nc.sbuf_tensor("tinyA", [128, NIC], F32))
        tinyB = E(nc.sbuf_tensor("tinyB", [128, NIC], F32))
        lsum = E(nc.sbuf_tensor("lsum", [128, 1], F32))
        ps_a = E(nc.psum_tensor("ps_a", [128, 512], F32))
        ps_b = E(nc.psum_tensor("ps_b", [128, 512], F32))
        ps_c = E(nc.psum_tensor("ps_c", [128, 512], F32))
        ps_d = E(nc.psum_tensor("ps_d", [128, 512], F32))
        ps_tr = E(nc.psum_tensor("ps_tr", [128, 1024], BF16))
        ps_tr2 = E(nc.psum_tensor("ps_tr2", [128, 1024], BF16))
        PSTR = (ps_tr, ps_tr2)
        sem_cst = E(nc.semaphore("sem_cst"))
        sem_x = [E(nc.semaphore("sem_x0")), E(nc.semaphore("sem_x1")),
                 E(nc.semaphore("sem_x2"))]
        sem_f1 = E(nc.semaphore("sem_f1"))
        sem_f2 = E(nc.semaphore("sem_f2"))
        sem_w = E(nc.semaphore("sem_w"))
        sem_wl = [E(nc.semaphore("sem_wl0")), E(nc.semaphore("sem_wl1"))]
        sem_wp = [E(nc.semaphore("sem_wp0")), E(nc.semaphore("sem_wp1"))]
        sem_qp = E(nc.semaphore("sem_qp"))
        sem_pk = [E(nc.semaphore("sem_pk0")), E(nc.semaphore("sem_pk1"))]
        sem_acc = [E(nc.semaphore("sem_acc0")), E(nc.semaphore("sem_acc1"))]
        sem_y = [E(nc.semaphore("sem_y0")), E(nc.semaphore("sem_y1"))]
        sem_sg = E(nc.semaphore("sem_sg"))
        sem_v = E(nc.semaphore("sem_v"))
        sem_s = E(nc.semaphore("sem_s"))
        sem_t = E(nc.semaphore("sem_t"))
        sem_pvv = E(nc.semaphore("sem_pvv"))
        sem_pvs = E(nc.semaphore("sem_pvs"))
        block = E(nc.Block())

        PS = {"a": ps_a, "b": ps_b, "c": ps_c, "d": ps_d}
        BANKS = ("a", "b", "c", "d")

        def sqw():
            return xn_f[:]

        def wld(i):
            return xw[:, i]

        def wcast(i):
            return bscr[:, i * D:(i + 1) * D]

        def xnT(dcc):
            return ereg[:, dcc * T:(dcc + 1) * T]

        def wbf(dcc):
            return ereg[:, DC * T + dcc * D: DC * T + (dcc + 1) * D]

        def e_chunk(jt, ic):
            fic = valid_chunks(jt)[0]
            o = E_OFF[jt] + (ic - fic) * 512
            return ereg[:, o:o + 512]

        def e_cols(jt, i0, w):
            ic = i0 // 512
            fic = valid_chunks(jt)[0]
            o = E_OFF[jt] + (ic - fic) * 512 + (i0 - ic * 512)
            return ereg[:, o:o + w]

        def q_bf():
            return bscr[:, 0:D]

        def xn_bf():
            return bscr[:, D:2 * D]

        def pstage(jt):
            return bscr[:, jt * 128:(jt + 1) * 128]

        def qt(kc):
            return qt_r[:, kc]

        def kt(kc):
            return kt_r[:, kc]

        def vt(jt):
            return v_r[:, jt]

        class St:
            cst = 0
            x = [0, 0, 0]
            f1 = 0
            f2 = 0
            w = 0
            wl = [0, 0]
            wp = [0, 0]
            qp = 0
            pk = [0, 0]
            acc = [0, 0]
            y = [0, 0]
            sg = 0
            v = 0
            s = 0
            t = 0
            pv_v = 0
            pv_s = 0
            bank_evac = {"a": None, "b": None, "c": None, "d": None}
            pstr_evac = [0, 0]

        prog = {k: [] for k in ("sync", "gpsimd", "vector", "scalar", "tensor")}

        _ENG_SEM = {"vector": (sem_v, "v"), "scalar": (sem_s, "s"), "tensor": (sem_t, "t")}

        def emit(engine, fn):
            if engine in _ENG_SEM:
                sem, attr = _ENG_SEM[engine]
                cnt = getattr(St, attr)

                def wrapped(eng, fn=fn, sem=sem, cnt=cnt):
                    if cnt:
                        eng.wait_ge(sem, cnt)
                    fn(eng)
                prog[engine].append(wrapped)
            else:
                prog[engine].append(fn)

        def c_loads(sync):
            sync.dma_start(out=ident_sb[:], in_=ident_h.ap()).then_inc(sem_cst, 16)
            sync.dma_start(out=band_sb[:], in_=band_h.ap()).then_inc(sem_cst, 16)
            sync.dma_start(out=ones_sb[:], in_=ones_h.ap()).then_inc(sem_cst, 16)
            for kc in range(DC):
                sync.dma_start(out=bq_sb[:, kc:kc + 1],
                               in_=bq_h.ap()[0:1, kc * 128:(kc + 1) * 128].rearrange("a b -> (a b)").unsqueeze(1)).then_inc(sem_cst, 16)
                sync.dma_start(out=bk_sb[:, kc:kc + 1],
                               in_=bk_h.ap()[0:1, kc * 128:(kc + 1) * 128].rearrange("a b -> (a b)").unsqueeze(1)).then_inc(sem_cst, 16)
            sync.dma_start(out=xn_f[0:1, :], in_=bv_h.ap()).then_inc(sem_cst, 16)
        emit("sync", c_loads)
        St.cst += (4 + 2 * DC) * 16
        cst1 = St.cst

        def c_cast1(v, need=cst1):
            v.wait_ge(sem_cst, need)
            v.tensor_copy(bv_row[:], xn_f[0:1, :]).then_inc(sem_v, 1)
        emit("vector", c_cast1)
        St.v += 1
        c_cast1_v = St.v

        def c_load2(sync, need=c_cast1_v):
            sync.wait_ge(sem_v, need)
            sync.dma_start(out=xn_f[0:1, :], in_=bf_h.ap()).then_inc(sem_cst, 16)
        emit("sync", c_load2)
        St.cst += 16
        cst2 = St.cst

        def c_cast2(v, need=cst2):
            v.wait_ge(sem_cst, need)
            v.tensor_copy(bf_row[:], xn_f[0:1, :]).then_inc(sem_v, 1)
        emit("vector", c_cast2)
        St.v += 1
        c_cast2_v = St.v

        cast_done = {}
        for wi, wsrc in enumerate((wq_h, wk_h, wv_h)):
            for tc in range(DC):
                k = wi * DC + tc
                sl = slice(tc * 128, (tc + 1) * 128)
                p = k % 2

                def ld(sync, wsrc=wsrc, sl=sl, k=k, p=p):
                    if k >= 2:
                        sync.wait_ge(sem_v, cast_done[k - 2])
                    else:
                        sync.wait_ge(sem_v, c_cast1_v)
                    sync.dma_start(out=wld(p), in_=wsrc.ap()[sl, :]).then_inc(sem_wl[p], 16)
                emit("sync", ld)
                St.wl[p] += 16
                need_wl = St.wl[p]

                def cast(v, k=k, p=p, need_wl=need_wl, need_wp=St.wp[p]):
                    v.wait_ge(sem_wl[p], need_wl)
                    if need_wp:
                        v.wait_ge(sem_wp[p], need_wp)
                    v.tensor_copy(wcast(p), wld(p)).then_inc(sem_v, 1)
                emit("vector", cast)
                St.v += 1
                cast_done[k] = St.v
                need_v = St.v

                def park(g, sl=sl, k=k, wi=wi, p=p, need_v=need_v):
                    g.wait_ge(sem_v, need_v)
                    g.dma_start(out=wqkv_bf.ap()[wi, sl, :], in_=wcast(p)).then_inc(sem_wp[p], 16)
                emit("gpsimd", park)
                St.wp[p] += 16
        wp_final = (St.wp[0], St.wp[1])

        wamax_done = {}
        for tc in range(DC):
            sl = slice(tc * 128, (tc + 1) * 128)
            p = tc % 2

            def ldf(sync, sl=sl, tc=tc, p=p):
                if tc >= 2:
                    sync.wait_ge(sem_v, wamax_done[tc - 2])
                else:
                    sync.wait_ge(sem_v, cast_done[22 + tc])
                sync.dma_start(out=wld(p), in_=wf_h.ap()[sl, :]).then_inc(sem_wl[p], 16)
            emit("sync", ldf)
            St.wl[p] += 16
            need_wl = St.wl[p]

            def red(v, tc=tc, p=p, need_wl=need_wl):
                v.wait_ge(sem_wl[p], need_wl)
                if tc == 0:
                    v.reduce_max(amax_acc[:], wld(0), axis=AX.X,
                                 apply_absolute_value=True).then_inc(sem_v, 1)
                else:
                    v.reduce_max(t1_sb[:], wld(p), axis=AX.X,
                                 apply_absolute_value=True).then_inc(sem_v, 1)
                    v.tensor_max(amax_acc[:], amax_acc[:], t1_sb[:]).then_inc(sem_v, 1)
            emit("vector", red)
            St.v += 1 if tc == 0 else 2
            wamax_done[tc] = St.v
        wamax_v = St.v

        def wam_x(g, need=wamax_v):
            g.wait_ge(sem_v, need)
            g.dma_start(out=row_sc[:], in_=amax_acc[:, 0:1]).then_inc(sem_sg, 16)
        emit("gpsimd", wam_x)
        St.sg += 16
        need_sg = St.sg

        def wam_r(v, need=need_sg):
            v.wait_ge(sem_sg, need)
            v.reduce_max(row_sc[0:1, 0:1], row_sc[:], axis=AX.X).then_inc(sem_v, 1)
        emit("vector", wam_r)
        St.v += 1
        need_v = St.v

        def wam_b(g, need=need_v):
            g.wait_ge(sem_v, need)
            g.dma_start(out=bounce.ap()[0:1, 0:1], in_=row_sc[0:1, 0:1]).then_inc(sem_sg, 16)
        emit("gpsimd", wam_b)
        St.sg += 16
        need_sg = St.sg

        def wam_bc(sync, need=need_sg, gate=c_cast2_v):
            sync.wait_ge(sem_v, gate)
            sync.wait_ge(sem_sg, need)
            sync.dma_start(out=swq_sb[:, 0:1],
                           in_=bounce.ap()[0:1, 0:1].partition_broadcast(128)).then_inc(sem_cst, 16)
        emit("sync", wam_bc)
        St.cst += 16
        cst3 = St.cst

        def wscale(v, need=cst3, base=St.v):
            v.wait_ge(sem_cst, need)
            v.tensor_scalar(t1_sb[:], swq_sb[:, 0:1], 1e-12, None, ALU.max).then_inc(sem_v, 1)
            v.tensor_tensor(thrw_sb[:], t1_sb[:], t1_sb[:], ALU.mult).then_inc(sem_v, 1)
            v.tensor_scalar(thrw_sb[:], thrw_sb[:], 4.0 / 49.0, None, ALU.mult).then_inc(sem_v, 1)
            v.wait_ge(sem_v, base + 1)
            v.reciprocal(t2_sb[:], t1_sb[:]).then_inc(sem_v, 1)
            v.tensor_scalar(swq_sb[:], t2_sb[:], 448.0, None, ALU.mult).then_inc(sem_v, 1)
            v.wait_ge(sem_v, base + 4)
            v.tensor_scalar(swqh_sb[:], t2_sb[:], 224.0, None, ALU.mult).then_inc(sem_v, 1)
            v.tensor_scalar(cpw_sb[:], t1_sb[:], 240.0 / 448.0 * 0.99999, None, ALU.mult).then_inc(sem_v, 1)
            v.tensor_scalar(cnw_sb[:], cpw_sb[:], -1.0, None, ALU.mult).then_inc(sem_v, 1)
        emit("vector", wscale)
        St.v += 8
        wscale_v = St.v

        wq_vdone = {}
        wq_tdone = {}
        wq_qp = {}
        wq_msk_v = {}
        for tc in range(DC):
            sl = slice(tc * 128, (tc + 1) * 128)
            p = tc % 2

            def ldf2(sync, sl=sl, tc=tc, p=p):
                if tc >= 2:
                    sync.wait_ge(sem_v, wq_vdone[tc - 2])
                else:
                    sync.wait_ge(sem_v, wamax_done[6 + tc])
                sync.dma_start(out=wld(p), in_=wf_h.ap()[sl, :]).then_inc(sem_wl[p], 16)
            emit("sync", ldf2)
            St.wl[p] += 16
            need_wl = St.wl[p]

            def wclip(v, tc=tc, p=p, need_wl=need_wl, need_v=wscale_v):
                v.wait_ge(sem_wl[p], need_wl)
                if tc == 0:
                    v.wait_ge(sem_v, need_v)
                v.tensor_scalar(accw[:, 0], wld(p), cpw_sb[:, 0:1], cnw_sb[:, 0:1],
                                ALU.min, ALU.max).then_inc(sem_v, 1)
            emit("vector", wclip)
            St.v += 1
            wclip_v = St.v

            def q_s(s, tc=tc, p=p, need_wl=need_wl, need_clip=wclip_v,
                    need_qv=(wq_vdone.get(tc - 1) if tc >= 1 else None)):
                s.wait_ge(sem_wl[p], need_wl)
                s.wait_ge(sem_v, need_clip)
                if need_qv is not None:
                    s.wait_ge(sem_v, need_qv)
                s.activation(lo8_t[:], accw[:, 0], AF.Copy, scale=swq_sb[:, 0:1]).then_inc(sem_s, 1)
                s.activation(hi8_t[:], wld(p), AF.Copy, scale=swqh_sb[:, 0:1]).then_inc(sem_s, 1)
            emit("scalar", q_s)
            St.s += 2
            need_s = St.s

            wq_msk_v[tc] = St.v + 2
            def q_v(v, tc=tc, p=p, need_s=need_s, wpf=wp_final):
                v.wait_ge(sem_s, need_s)
                if tc == 0:
                    v.wait_ge(sem_wp[0], wpf[0])
                    v.wait_ge(sem_wp[1], wpf[1])
                if tc >= 1:
                    v.wait_ge(sem_t, wq_tdone[tc - 1])
                v.scalar_tensor_tensor(hi2_t[:], wld(p), 1.0, wld(p),
                                       ALU.bypass, ALU.mult).then_inc(sem_v, 1)
                v.tensor_scalar(msk8[:], hi2_t[:], thrw_sb[:, 0:1], None,
                                ALU.is_gt).then_inc(sem_v, 1)
                v.tensor_scalar(hi2_t[:], hi8_t[:], 2.0, None, ALU.mult).then_inc(sem_v, 1)
                v.wait_ge(sem_v, wq_msk_v[tc])
                v.select(q_bf(), msk8[:], hi2_t[:], lo8_t[:], add_drain=True).then_inc(sem_v, 1)
            emit("vector", q_v)
            St.v += 4
            wq_vdone[tc] = St.v
            need_v = St.v

            def q_t(t, need_v=need_v, need_pstr=St.pstr_evac[0]):
                t.wait_ge(sem_v, need_v)
                if need_pstr:
                    t.wait_ge(sem_v, need_pstr)
                ins = None
                for blk in range(8):
                    ins = t.transpose(ps_tr[:, blk * 128:(blk + 1) * 128],
                                      q_bf()[:, blk * 128:(blk + 1) * 128], ident_sb[:])
                ins.then_inc(sem_t, 1)
            emit("tensor", q_t)
            St.t += 1
            wq_tdone[tc] = St.t
            need_t = St.t

            def q_e(v, tc=tc, need_t=need_t, need_qp=St.qp):
                v.wait_ge(sem_t, need_t)
                if need_qp:
                    v.wait_ge(sem_qp, need_qp)
                v.tensor_copy(xqT_st[:], ps_tr[:]).then_inc(sem_v, 1)
            emit("vector", q_e)
            St.v += 1
            St.pstr_evac[0] = St.v
            need_v = St.v

            def q_p(g, tc=tc, need_v=need_v):
                g.wait_ge(sem_v, need_v)
                for blk in range(8):
                    g.dma_start(out=w8T_d.ap()[blk * 128:(blk + 1) * 128,
                                               tc * 128:(tc + 1) * 128],
                                in_=xqT_st[:, blk * 128:(blk + 1) * 128]).then_inc(sem_qp, 16)
            emit("gpsimd", q_p)
            St.qp += 8 * 16
            wq_qp[tc] = St.qp
        qp_final = St.qp
        w2_end_t = St.t

        ph0_vdone = {}
        ph0_sdone = {}
        ph0_gate_s = St.s
        for gt in range(256):
            b, tt = divmod(gt, NT)
            rs = slice(tt * 128, (tt + 1) * 128)
            c = slice(gt, gt + 1)
            slot = gt % 4
            buf = xw[:, slot] if slot < 2 else accw[:, slot - 2]

            def ld0(sync, b=b, rs=rs, gt=gt, slot=slot, buf=buf):
                if gt >= 4:
                    sync.wait_ge(sem_v, ph0_vdone[gt - 4])
                    sync.wait_ge(sem_s, ph0_sdone[gt - 4])
                elif gt < 2:
                    sync.wait_ge(sem_v, wq_vdone[6 + gt])
                elif gt == 2:
                    sync.wait_ge(sem_s, ph0_gate_s)
                sync.dma_start(out=buf, in_=xf3[b, rs, :]).then_inc(
                    (sem_x[0], sem_x[1], sem_f1, sem_f2)[slot], 16)
            emit("sync", ld0)
            if slot == 0:
                St.x[0] += 16
                xsem, need_x = sem_x[0], St.x[0]
            elif slot == 1:
                St.x[1] += 16
                xsem, need_x = sem_x[1], St.x[1]
            elif slot == 2:
                St.f1 += 16
                xsem, need_x = sem_f1, St.f1
            else:
                St.f2 += 16
                xsem, need_x = sem_f2, St.f2

            def sq0(s, gt=gt, c=c, buf=buf, xsem=xsem, need_x=need_x):
                s.wait_ge(xsem, need_x)
                if gt == 0:
                    s.wait_ge(sem_v, c_cast2_v)
                s.activation(sqw(), buf, AF.Identity,
                             accum_out=fsum[:, c]).then_inc(sem_s, 1)
                s.activation(sqw(), buf, AF.Square,
                             accum_out=fsq[:, c]).then_inc(sem_s, 1)
            emit("scalar", sq0)
            St.s += 2
            ph0_sdone[gt] = St.s

            def st0(v, gt=gt, c=c, buf=buf, xsem=xsem, need_x=need_x):
                v.wait_ge(xsem, need_x)
                v.reduce_max(fmax[:, c], buf, axis=AX.X).then_inc(sem_v, 1)
                v.tensor_reduce(fmin[:, c], buf, AX.X, ALU.min).then_inc(sem_v, 1)
            emit("vector", st0)
            St.v += 2
            ph0_vdone[gt] = St.v

            if gt % 16 == 15:
                def warm(t, need_v=St.pstr_evac[0], gate=ph0_vdone[gt - 8]):
                    t.wait_ge(sem_v, need_v)
                    t.wait_ge(sem_v, gate)
                    t.transpose(ps_tr[:, 0:128], ident_sb[:], ident_sb[:]).then_inc(sem_t, 1)
                emit("tensor", warm)
                St.t += 1
        ph0_all_s = St.s

        slA = slice(0, 256)

        def comb1(v, need_s=ph0_all_s, base=St.v):
            v.wait_ge(sem_s, need_s)
            k = [base]

            def w(ins):
                k[0] += 1
                ins.then_inc(sem_v, 1)
                v.wait_ge(sem_v, k[0])
            w(v.tensor_scalar(tmp_f[:, slA], fsum[:, slA], -1.0 / D, None, ALU.mult))
            w(v.tensor_scalar(fsq[:, slA], fsq[:, slA], 1.0 / D, None, ALU.mult))
            w(v.scalar_tensor_tensor(fsum[:, slA], tmp_f[:, slA], 1.0, tmp_f[:, slA],
                                     ALU.bypass, ALU.mult))
            w(v.tensor_sub(fsq[:, slA], fsq[:, slA], fsum[:, slA]))
            w(v.tensor_scalar(fsq[:, slA], fsq[:, slA], 1e-5, None, ALU.add))
            w(v.reciprocal(fsum[:, slA], fsq[:, slA]))
        emit("vector", comb1)
        St.v += 6
        need_v = St.v

        def comb_sqrt(s, need=need_v):
            s.wait_ge(sem_v, need)
            s.activation(fsum[:, slA], fsum[:, slA], AF.Sqrt).then_inc(sem_s, 1)
        emit("scalar", comb_sqrt)
        St.s += 1
        need_s = St.s

        def comb2(v, need=need_s, base=St.v):
            v.wait_ge(sem_s, need)
            k = [base]

            def w(ins):
                k[0] += 1
                ins.then_inc(sem_v, 1)
                v.wait_ge(sem_v, k[0])
            for _ in range(2):
                w(v.scalar_tensor_tensor(tmp_g[:, slA], fsum[:, slA], 1.0, fsum[:, slA],
                                         ALU.bypass, ALU.mult))
                w(v.tensor_tensor(tmp_g[:, slA], tmp_g[:, slA], fsq[:, slA], ALU.mult))
                w(v.tensor_scalar(tmp_g[:, slA], tmp_g[:, slA], -0.5, 1.5, ALU.mult, ALU.add))
                w(v.tensor_tensor(fsum[:, slA], fsum[:, slA], tmp_g[:, slA], ALU.mult))
            w(v.tensor_tensor(fmax[:, slA], fmax[:, slA], tmp_f[:, slA], ALU.add))
            w(v.tensor_tensor(fmin[:, slA], fmin[:, slA], tmp_f[:, slA], ALU.add))
            w(v.tensor_scalar(fmin[:, slA], fmin[:, slA], -1.0, None, ALU.mult))
            w(v.tensor_max(fmax[:, slA], fmax[:, slA], fmin[:, slA]))
            w(v.tensor_tensor(fmax[:, slA], fmax[:, slA], fsum[:, slA], ALU.mult))
            w(v.reduce_max(amax_acc[:], fmax[:, slA], axis=AX.X))
            w(v.tensor_scalar(tmp_g[:, slA], fsum[:, slA], 1.0, None, ALU.add))
            w(v.tensor_tensor(fmin[:, slA], tmp_f[:, slA], fsum[:, slA], ALU.mult))
        emit("vector", comb2)
        St.v += 16
        amax_v = St.v

        cid = nc.partition_id([mybir.EngineType.Pool])

        def own_cp(g, need=amax_v):
            g.wait_ge(sem_v, need)
            g.dma_start(out=mneg_st[:, 0:2 * NT], in_=tmp_f[:, bass.ds(cid * 32, 32)]).then_inc(sem_sg, 16)
            g.dma_start(out=r_st[:, 0:2 * NT], in_=fsum[:, bass.ds(cid * 32, 32)]).then_inc(sem_sg, 16)
            g.dma_start(out=s1p_st[:, 0:2 * NT], in_=tmp_g[:, bass.ds(cid * 32, 32)]).then_inc(sem_sg, 16)
            g.dma_start(out=s2p_st[:, 0:2 * NT], in_=fmin[:, bass.ds(cid * 32, 32)]).then_inc(sem_sg, 16)
        emit("gpsimd", own_cp)
        St.sg += 64

        def am_x(g, need=amax_v):
            g.wait_ge(sem_v, need)
            g.dma_start(out=row_sc[:], in_=amax_acc[:, 0:1]).then_inc(sem_sg, 16)
        emit("gpsimd", am_x)
        St.sg += 16
        own_sg = St.sg
        need_sg = St.sg

        def am_r(v, need=need_sg):
            v.wait_ge(sem_sg, need)
            v.reduce_max(row_sc[0:1, 0:1], row_sc[:], axis=AX.X).then_inc(sem_v, 1)
        emit("vector", am_r)
        St.v += 1
        need_v = St.v

        def am_b(g, need=need_v):
            g.wait_ge(sem_v, need)
            g.dma_start(out=bounce.ap()[0:1, 1:2], in_=row_sc[0:1, 0:1]).then_inc(sem_sg, 16)
        emit("gpsimd", am_b)
        St.sg += 16
        need_sg = St.sg

        def am_bc(sync, need=need_sg, gate=wscale_v):
            sync.wait_ge(sem_v, gate)
            sync.wait_ge(sem_sg, need)
            sync.dma_start(out=amax_bc[:, 0:1],
                           in_=bounce.ap()[0:1, 1:2].partition_broadcast(128)).then_inc(sem_cst, 16)
        emit("sync", am_bc)
        St.cst += 16
        cst4 = St.cst

        def gscale(v, need=cst4, own=own_sg, base=St.v):
            v.wait_ge(sem_cst, need)
            v.wait_ge(sem_sg, own)
            k = [base]

            def w(ins):
                k[0] += 1
                ins.then_inc(sem_v, 1)
                v.wait_ge(sem_v, k[0])
            w(v.tensor_scalar(t1_sb[:], amax_bc[:, 0:1], 1e-12, None, ALU.max))
            w(v.tensor_tensor(thr_sb[:], t1_sb[:], t1_sb[:], ALU.mult))
            w(v.tensor_scalar(thr_sb[:], thr_sb[:], 4.0 / 49.0, None, ALU.mult))
            w(v.reciprocal(t2_sb[:], t1_sb[:]))
            w(v.tensor_scalar(sx_sb[:], t2_sb[:], 448.0, None, ALU.mult))
            w(v.tensor_scalar(sxh_sb[:], t2_sb[:], 224.0, None, ALU.mult))
            w(v.tensor_scalar(cpx_sb[:], t1_sb[:], 240.0 / 448.0 * 0.99999, None, ALU.mult))
            w(v.tensor_scalar(cnx_sb[:], cpx_sb[:], -1.0, None, ALU.mult))
            w(v.tensor_tensor(sprod_sb[:], sx_sb[:], swq_sb[:], ALU.mult))
            w(v.reciprocal(scinv_sb[:], sprod_sb[:]))
            w(v.tensor_scalar(bf_rs[:], bf_row[:], sprod_sb[0:1, 0:1], None, ALU.mult))
        emit("vector", gscale)
        St.v += 11
        gscale_v = St.v

        last_wbf_reader_t = 0
        last_resc_v = 0
        last_pv_t = 0
        acc_parked = {}
        out_done = {}
        for b in range(B_LOC):
            for tc in range(DC):
                def ldw8(sync, tc=tc, b=b, need_qp=qp_final,
                         need_t=last_wbf_reader_t, need_v=last_resc_v):
                    if tc == 0:
                        if b == 0:
                            sync.wait_ge(sem_qp, need_qp)
                        if need_t:
                            sync.wait_ge(sem_t, need_t)
                        if need_v:
                            sync.wait_ge(sem_v, need_v)
                    sync.dma_start(out=wbf(tc),
                                   in_=w8T_d.ap()[tc * 128:(tc + 1) * 128, :]).then_inc(sem_w, 16)
                emit("sync", ldw8)
                St.w += 16
            w8T_in = St.w

            tr2_done = {}
            for tt in range(NT):
                gt = b * NT + tt
                rs = slice(tt * 128, (tt + 1) * 128)
                c = slice(gt, gt + 1)
                p = tt % 2

                px = tt % 3

                def ld1(sync, b=b, rs=rs, tt=tt, px=px,
                        gate_pk=((tt - 3) % 2 if tt >= 3 else (NT - 3 + tt) % 2,
                                 acc_parked.get((b, tt - 3)) if tt >= 3 else
                                 acc_parked.get((b - 1, NT - 3 + tt))),
                        gate_v=(ph0_vdone[252 + tt] if (b == 0 and tt < 2) else None),
                        gate_s=(ph0_sdone[252 + tt] if (b == 0 and tt < 2) else None)):
                    if gate_pk[1] is not None:
                        sync.wait_ge(sem_pk[gate_pk[0]], gate_pk[1])
                    if gate_v is not None:
                        sync.wait_ge(sem_v, gate_v)
                    if gate_s is not None:
                        sync.wait_ge(sem_s, gate_s)
                    sync.dma_start(out=xw[:, px], in_=x3[b, rs, :]).then_inc(sem_x[px], 16)
                emit("sync", ld1)
                St.x[px] += 16
                need_x = St.x[px]

                def xnS(s, c=c, px=px, need_x=need_x,
                        first=(b == 0 and tt == 0), gv=gscale_v):
                    s.wait_ge(sem_x[px], need_x)
                    if first:
                        s.wait_ge(sem_v, gv)
                    s.activation(xn_f[:], xw[:, px], AF.Identity,
                                 scale=r_st[:, c], bias=s2p_st[:, c]).then_inc(sem_s, 1)
                emit("scalar", xnS)
                St.s += 1
                need_xs = St.s

                def xn1(v, p=p, need_xs=need_xs,
                        need_pk=(acc_parked.get((b, tt - 2))),
                        need_y=(out_done.get((b - 1, NT - 2 + tt))
                                if (b >= 1 and tt < 2) else None)):
                    v.wait_ge(sem_s, need_xs)
                    if need_pk is not None:
                        v.wait_ge(sem_pk[p], need_pk)
                    if need_y is not None:
                        v.wait_ge(sem_y[p], need_y)
                    v.scalar_tensor_tensor(hi2_t[:], xn_f[:], 1.0, xn_f[:],
                                           ALU.bypass, ALU.mult).then_inc(sem_v, 1)
                    v.tensor_scalar(msk8[:], hi2_t[:], thr_sb[:, 0:1], None,
                                    ALU.is_gt).then_inc(sem_v, 1)
                    v.tensor_scalar(accw[:, p], xn_f[:], cpx_sb[:, 0:1], cnx_sb[:, 0:1],
                                    ALU.min, ALU.max).then_inc(sem_v, 1)
                emit("vector", xn1)
                St.v += 3
                need_v = St.v

                def q1s(s, tt=tt, b=b, c=c, p=p, px=px, need_v=need_v,
                        need_t=(tr2_done.get(tt - 1) if tt >= 1 else
                                (w2_end_t if b == 0 else last_pv_t)),
                        need_pk=(acc_parked.get((b, tt - 2))),
                        need_y=(out_done.get((b - 1, NT - 2 + tt))
                                if (b >= 1 and tt < 2) else None)):
                    s.wait_ge(sem_v, need_v)
                    if need_t:
                        s.wait_ge(sem_t, need_t)
                    if need_pk is not None:
                        s.wait_ge(sem_pk[p], need_pk)
                    if need_y is not None:
                        s.wait_ge(sem_y[p], need_y)
                    s.activation(lo8_t[:], accw[:, p], AF.Copy, scale=sx_sb[:, 0:1]).then_inc(sem_s, 1)
                    s.activation(hi8_t[:], xn_f[:], AF.Copy, scale=sxh_sb[:, 0:1]).then_inc(sem_s, 1)
                    s.activation(xn_bf(), xn_f[:], AF.Copy).then_inc(sem_s, 1)
                    s.activation(accw[:, p], xw[:, px], AF.Identity,
                                 scale=s1p_st[:, c], bias=s2p_st[:, c]).then_inc(sem_s, 1)
                emit("scalar", q1s)
                St.s += 4
                acc0_s = St.s
                need_s = St.s

                def q1v(v, need_s=need_s,
                        need_t=(tr2_done.get(tt - 1) if tt >= 1 else
                                (w2_end_t if b == 0 else last_pv_t))):
                    v.wait_ge(sem_s, need_s)
                    if need_t:
                        v.wait_ge(sem_t, need_t)
                    v.tensor_scalar(hi2_t[:], hi8_t[:], 2.0, None, ALU.mult).then_inc(sem_v, 1)
                    v.select(q_bf(), msk8[:], hi2_t[:], lo8_t[:], add_drain=True).then_inc(sem_v, 1)
                emit("vector", q1v)
                St.v += 2
                need_v = St.v

                def tr1(t, need_v=need_v, need_pstr=St.pstr_evac[0]):
                    t.wait_ge(sem_v, need_v)
                    t.wait_ge(sem_v, need_pstr)
                    ins = None
                    for blk in range(8):
                        ins = t.transpose(ps_tr[:, blk * 128:(blk + 1) * 128],
                                          xn_bf()[:, blk * 128:(blk + 1) * 128], ident_sb[:])
                    ins.then_inc(sem_t, 1)
                emit("tensor", tr1)
                St.t += 1
                need_t = St.t

                def ev1(v, rs=rs, need_t=need_t):
                    v.wait_ge(sem_t, need_t)
                    for blk in range(8):
                        v.tensor_copy(xnT(blk)[:, rs],
                                      ps_tr[:, blk * 128:(blk + 1) * 128]).then_inc(sem_v, 1)
                emit("vector", ev1)
                St.v += 8
                St.pstr_evac[0] = St.v

                def tr2(t, need_v=need_v, need_pstr=St.pstr_evac[1]):
                    t.wait_ge(sem_v, need_v)
                    if need_pstr:
                        t.wait_ge(sem_v, need_pstr)
                    ins = None
                    for blk in range(8):
                        ins = t.transpose(ps_tr2[:, blk * 128:(blk + 1) * 128],
                                          q_bf()[:, blk * 128:(blk + 1) * 128], ident_sb[:])
                    ins.then_inc(sem_t, 1)
                emit("tensor", tr2)
                St.t += 1
                tr2_done[tt] = St.t
                need_t = St.t

                def ev2(v, b=b, tt=tt, need_t=need_t, need_qp=qp_final):
                    v.wait_ge(sem_t, need_t)
                    if b == 0 and tt == 0:
                        v.wait_ge(sem_qp, need_qp)
                    v.tensor_copy(xqT_st[:], ps_tr2[:]).then_inc(sem_v, 1)
                emit("vector", ev2)
                St.v += 1
                St.pstr_evac[1] = St.v
                xq_ready_v = St.v

                for oc, bank in ((0, BANKS[(2 * tt + 0) % 4]), (1, BANKS[(2 * tt + 1) % 4])):
                    osl = slice(oc * 512, (oc + 1) * 512)

                    def ffn_mm(t, oc=oc, osl=osl, bank=bank, tt=tt,
                               need_v=xq_ready_v, w_in=w8T_in,
                               bank_free=St.bank_evac[bank]):
                        t.wait_ge(sem_v, need_v)
                        if tt == 0 and oc == 0:
                            t.wait_ge(sem_w, w_in)
                        if bank_free:
                            t.wait_ge(sem_pvv if bank_free[0] == "v" else sem_pvs,
                                      bank_free[1])
                        for dcc in range(DC):
                            t.matmul(PS[bank][:, :],
                                     xqT_st[:, dcc * 128:(dcc + 1) * 128],
                                     wbf(dcc)[:, osl],
                                     start=(dcc == 0), stop=False)
                        t.matmul(PS[bank][:, :], ones_sb[0:1, :], bf_rs[0:1, osl],
                                 start=False, stop=True).then_inc(sem_t, 1)
                    emit("tensor", ffn_mm)
                    St.t += 1
                    last_wbf_reader_t = St.t
                    need_t = St.t

                    def ffn_ev(v, osl=osl, bank=bank, p=p,
                               need_t=need_t, need_s=acc0_s):
                        v.wait_ge(sem_t, need_t)
                        v.wait_ge(sem_s, need_s)
                        v.scalar_tensor_tensor(accw[:, p, osl.start:osl.stop],
                                               PS[bank][:, :], scinv_sb[:, 0:1],
                                               accw[:, p, osl.start:osl.stop],
                                               ALU.mult, ALU.add).then_inc(sem_v, 1)
                        v.engine_nop().then_inc(sem_pvv, 1)
                    emit("vector", ffn_ev)
                    St.v += 1
                    St.pv_v += 1
                    St.bank_evac[bank] = ("v", St.pv_v)

                def park1(g, b=b, rs=rs, p=p, need_v=St.v):
                    g.wait_ge(sem_v, need_v)
                    g.dma_start(out=acc3[b, rs, :], in_=accw[:, p]).then_inc(sem_pk[p], 16)
                emit("gpsimd", park1)
                St.pk[p] += 16
                acc_parked[(b, tt)] = St.pk[p]

            xnT_all_v = St.v

            for name, wi, biascol in (("q", 0, bq_sb), ("k", 1, bk_sb), ("v", 2, None)):
                for tc in range(DC):
                    def ldq(sync, wi=wi, tc=tc, name=name, b=b,
                            need_t=last_wbf_reader_t, wpf=wp_final):
                        if tc == 0:
                            sync.wait_ge(sem_t, need_t)
                            if name == "q" and b == 0:
                                sync.wait_ge(sem_wp[0], wpf[0])
                                sync.wait_ge(sem_wp[1], wpf[1])
                        sync.dma_start(out=wbf(tc),
                                       in_=wqkv_bf.ap()[wi, tc * 128:(tc + 1) * 128, :]).then_inc(sem_w, 16)
                    emit("sync", ldq)
                    St.w += 16
                w_in = St.w

                if name in ("q", "k"):
                    dst = qt if name == "q" else kt
                    for kc in range(DC):
                        for ich in range(NIC):
                            isl = slice(ich * 512, (ich + 1) * 512)
                            bank = BANKS[(kc * NIC + ich) % 4]

                            def qk_mm(t, kc=kc, isl=isl, bank=bank, w_in=w_in,
                                      first=(kc == 0 and ich == 0),
                                      need_v=xnT_all_v,
                                      bank_free=St.bank_evac[bank]):
                                if first:
                                    t.wait_ge(sem_w, w_in)
                                    t.wait_ge(sem_v, need_v)
                                if bank_free:
                                    t.wait_ge(sem_pvv if bank_free[0] == "v" else sem_pvs,
                                              bank_free[1])
                                ins = None
                                for dcc in range(DC):
                                    ins = t.matmul(PS[bank][:, :],
                                                   wbf(dcc)[:, kc * 128:(kc + 1) * 128],
                                                   xnT(dcc)[:, isl],
                                                   start=(dcc == 0), stop=(dcc == DC - 1))
                                ins.then_inc(sem_t, 1)
                            emit("tensor", qk_mm)
                            St.t += 1
                            last_wbf_reader_t = St.t
                            need_t = St.t

                            def qk_ev(s, kc=kc, isl=isl, bank=bank, dst=dst,
                                      biascol=biascol, need_t=need_t):
                                s.wait_ge(sem_t, need_t)
                                s.activation(dst(kc)[:, isl], PS[bank][:, :], AF.Identity,
                                             bias=biascol[:, kc:kc + 1]).then_inc(sem_s, 1)
                                s.nop().then_inc(sem_pvs, 1)
                            emit("scalar", qk_ev)
                            St.s += 1
                            St.pv_s += 1
                            St.bank_evac[bank] = ("s", St.pv_s)
                else:
                    for jt in range(NT):
                        js = slice(jt * 128, (jt + 1) * 128)
                        for vc in range(2):
                            vsl = slice(vc * 512, (vc + 1) * 512)
                            bank = BANKS[(jt * 2 + vc) % 4]

                            def v_mm(t, js=js, vsl=vsl, bank=bank, w_in=w_in,
                                     first=(jt == 0 and vc == 0),
                                     bank_free=St.bank_evac[bank]):
                                if first:
                                    t.wait_ge(sem_w, w_in)
                                if bank_free:
                                    t.wait_ge(sem_pvv if bank_free[0] == "v" else sem_pvs,
                                              bank_free[1])
                                for dcc in range(DC):
                                    t.matmul(PS[bank][:, :],
                                             xnT(dcc)[:, js],
                                             wbf(dcc)[:, vsl],
                                             start=(dcc == 0), stop=False)
                                t.matmul(PS[bank][:, :], ones_sb[0:1, :], bv_row[0:1, vsl],
                                         start=False, stop=True).then_inc(sem_t, 1)
                            emit("tensor", v_mm)
                            St.t += 1
                            last_wbf_reader_t = St.t
                            need_t = St.t

                            def v_ev(s, jt=jt, vsl=vsl, bank=bank, need_t=need_t):
                                s.wait_ge(sem_t, need_t)
                                s.activation(vt(jt)[:, vsl], PS[bank][:, :], AF.Copy).then_inc(sem_s, 1)
                                s.nop().then_inc(sem_pvs, 1)
                            emit("scalar", v_ev)
                            St.s += 1
                            St.pv_s += 1
                            St.bank_evac[bank] = ("s", St.pv_s)

            for jt in range(NT):
                j0 = jt * 128
                js = slice(j0, j0 + 128)
                vcs = valid_chunks(jt)
                for n_i, ic in enumerate(vcs):
                    isl = slice(ic * 512, (ic + 1) * 512)
                    bank = BANKS[sum(len(valid_chunks(j)) for j in range(jt)) % 4 if False else (jt * NIC + ic) % 4]

                    def s_mm(t, js=js, isl=isl, bank=bank,
                             bank_free=St.bank_evac[bank]):
                        if bank_free:
                            t.wait_ge(sem_pvv if bank_free[0] == "v" else sem_pvs,
                                      bank_free[1])
                        ins = None
                        for kc in range(DC):
                            ins = t.matmul(PS[bank][:, :],
                                           kt(kc)[:, js],
                                           qt(kc)[:, isl],
                                           start=(kc == 0), stop=(kc == DC - 1))
                        ins.then_inc(sem_t, 1)
                    emit("tensor", s_mm)
                    St.t += 1
                    need_t = St.t

                    delta = j0 - ic * 512
                    partial = delta > -127

                    mrun_v = St.v + (3 if partial else 2)
                    def s_st(v, n_i=n_i, bank=bank, partial=partial, delta=delta,
                             need_t=need_t, mrun_v=mrun_v):
                        v.wait_ge(sem_t, need_t)
                        if partial:
                            off = 512 - delta
                            v.tensor_add(PS[bank][:, :], PS[bank][:, :],
                                         band_sb[:, off:off + 512]).then_inc(sem_v, 1)
                        v.reduce_max(tinyA[:, n_i:n_i + 1], PS[bank][:, :], axis=AX.X).then_inc(sem_v, 1)
                        if n_i == 0:
                            v.tensor_copy(mrun[:], tinyA[:, 0:1]).then_inc(sem_v, 1)
                        else:
                            v.tensor_max(mrun[:], mrun[:], tinyA[:, n_i:n_i + 1]).then_inc(sem_v, 1)
                        v.tensor_copy(msave[:, n_i:n_i + 1], mrun[:]).then_inc(sem_v, 1)
                        v.wait_ge(sem_v, mrun_v)
                        v.tensor_scalar(mnegs[:, n_i:n_i + 1], mrun[:], -1.0, None,
                                        ALU.mult).then_inc(sem_v, 1)
                    emit("vector", s_st)
                    St.v += 5 if partial else 4
                    need_v = St.v

                    def s_exp(s, jt=jt, ic=ic, bank=bank, n_i=n_i, need_v=need_v):
                        s.wait_ge(sem_v, need_v)
                        s.activation(e_chunk(jt, ic), PS[bank][:, :], AF.Exp,
                                     bias=mnegs[:, n_i:n_i + 1],
                                     accum_out=ssave[:, n_i:n_i + 1]).then_inc(sem_s, 1)
                        s.nop().then_inc(sem_pvs, 1)
                    emit("scalar", s_exp)
                    St.s += 1
                    St.pv_s += 1
                    St.bank_evac[bank] = ("s", St.pv_s)
                exp_done_s = St.s
                nv = len(vcs)

                def fin_v(v, nv=nv, need_s=exp_done_s, base=St.v):
                    v.wait_ge(sem_s, need_s)
                    k = [base]

                    def w(ins):
                        k[0] += 1
                        ins.then_inc(sem_v, 1)
                        v.wait_ge(sem_v, k[0])
                    w(v.reduce_max(mrun[:], tinyA[:, 0:nv], axis=AX.X))
                    w(v.tensor_scalar(tinyA[:, 0:nv], tinyA[:, 0:nv], mrun[:, 0:1], None,
                                      ALU.subtract))
                emit("vector", fin_v)
                St.v += 2
                need_v = St.v

                def fin_s(s, nv=nv, need_v=need_v):
                    s.wait_ge(sem_v, need_v)
                    s.activation(tinyB[:, 0:nv], tinyA[:, 0:nv], AF.Exp).then_inc(sem_s, 1)
                emit("scalar", fin_s)
                St.s += 1
                need_s = St.s

                def fin_v2(v, jt=jt, nv=nv, need_s=need_s):
                    v.wait_ge(sem_s, need_s)
                    v.tensor_tensor(tinyA[:, 0:nv], tinyB[:, 0:nv], ssave[:, 0:nv], ALU.mult).then_inc(sem_v, 1)
                    v.reduce_sum(lsum[:], tinyA[:, 0:nv], axis=AX.X).then_inc(sem_v, 1)
                    v.reciprocal(lsum[:], lsum[:]).then_inc(sem_v, 1)
                    v.tensor_scalar(lsum[:], lsum[:], 1.0 / 32.0, None, ALU.mult).then_inc(sem_v, 1)
                    v.tensor_scalar(gall[:, jt, 0:nv], tinyB[:, 0:nv], lsum[:, 0:1], None,
                                    ALU.mult).then_inc(sem_v, 1)
                emit("vector", fin_v2)
                St.v += 5

            pv_mm_t = {}
            for it in range(NT):
                i0 = it * 128
                icov = i0 // 512
                p = it % 2

                def acc_ld(g, b=b, rs=slice(i0, i0 + 128), it=it, p=p,
                           need_y=out_done.get((b, it - 2)),
                           need_pk=(acc_parked.get((b, NT - 2 + it)) if it < 2 else None)):
                    if need_y is not None:
                        g.wait_ge(sem_y[p], need_y)
                    if need_pk is not None:
                        g.wait_ge(sem_pk[p], need_pk)
                    g.dma_start(out=accw[:, p], in_=acc3[b, rs, :]).then_inc(sem_acc[p], 16)
                emit("gpsimd", acc_ld)
                St.acc[p] += 16
                acc_in = St.acc[p]

                def resc(v, it=it, i0=i0, icov=icov,
                         need_t=pv_mm_t.get(it - 1) if it >= 1 else None):
                    if need_t is not None:
                        v.wait_ge(sem_t, need_t)
                    for jt in range(it + 1):
                        gcol = icov - valid_chunks(jt)[0]
                        v.tensor_scalar(pstage(jt),
                                        e_cols(jt, i0, 128),
                                        gall[:, jt, gcol:gcol + 1], None,
                                        ALU.mult).then_inc(sem_v, 1)
                emit("vector", resc)
                St.v += it + 1
                last_resc_v = St.v
                need_v = St.v

                for vc, bank in ((0, BANKS[(2 * it + 0) % 4]), (1, BANKS[(2 * it + 1) % 4])):
                    vsl = slice(vc * 512, (vc + 1) * 512)

                    def pv_mm(t, it=it, vsl=vsl, bank=bank, need_v=need_v,
                              bank_free=St.bank_evac[bank]):
                        t.wait_ge(sem_v, need_v)
                        if bank_free:
                            t.wait_ge(sem_pvv if bank_free[0] == "v" else sem_pvs,
                                      bank_free[1])
                        ins = None
                        for jt in range(it + 1):
                            ins = t.matmul(PS[bank][:, :],
                                           pstage(jt),
                                           vt(jt)[:, vsl],
                                           start=(jt == 0), stop=(jt == it))
                        ins.then_inc(sem_t, 1)
                    emit("tensor", pv_mm)
                    St.t += 1
                    need_t_mm = St.t

                    def merge(v, vsl=vsl, bank=bank, p=p, need_t=need_t_mm,
                              need_acc=acc_in):
                        v.wait_ge(sem_t, need_t)
                        v.wait_ge(sem_acc[p], need_acc)
                        v.tensor_add(accw[:, p, vsl.start:vsl.stop],
                                     accw[:, p, vsl.start:vsl.stop],
                                     PS[bank][:, :]).then_inc(sem_v, 1)
                        v.engine_nop().then_inc(sem_pvv, 1)
                    emit("vector", merge)
                    St.v += 1
                    St.pv_v += 1
                    St.bank_evac[bank] = ("v", St.pv_v)
                pv_mm_t[it] = St.t
                last_pv_t = St.t
                merged_v = St.v

                def out_wr(sync, b=b, rs=slice(i0, i0 + 128), p=p, need_v=merged_v):
                    sync.wait_ge(sem_v, need_v)
                    sync.dma_start(out=y3[b, rs, :], in_=accw[:, p]).then_inc(sem_y[p], 16)
                emit("sync", out_wr)
                St.y[p] += 16
                out_done[(b, it)] = St.y[p]

        yf = (St.y[0], St.y[1])

        def fin(g, yf=yf):
            g.wait_ge(sem_y[0], yf[0])
            g.wait_ge(sem_y[1], yf[1])
        emit("gpsimd", fin)

        class _SerIns:
            def __init__(self, ins, proxy):
                self._ins = ins
                self._proxy = proxy

            def then_inc(self, sem, n):
                self._ins.then_inc(sem, n)
                if sem is self._proxy._own:
                    self._proxy._n += n
                return self

        class _SerEng:
            _OPS = frozenset((
                "tensor_scalar", "tensor_tensor", "tensor_add", "tensor_sub",
                "tensor_max", "tensor_copy", "reduce_max", "reduce_sum",
                "tensor_reduce", "reciprocal", "select", "scalar_tensor_tensor",
                "engine_nop", "memset", "memzero", "activation", "nop",
                "tensor_single_scalar", "copy", "add", "mul", "square", "sqrt",
            ))

            def __init__(self, eng, own_sem):
                self._eng = eng
                self._own = own_sem
                self._n = 0

            def wait_ge(self, sem, val):
                return self._eng.wait_ge(sem, val)

            def __getattr__(self, name):
                target = getattr(self._eng, name)
                if name in _SerEng._OPS:
                    def call(*a, _t=target, **k):
                        if self._n:
                            self._eng.wait_ge(self._own, self._n)
                        return _SerIns(_t(*a, **k), self)
                    return call
                return target

        @block.sync
        def _(sync):
            for fn in prog["sync"]:
                fn(sync)

        @block.gpsimd
        def _(g):
            for fn in prog["gpsimd"]:
                fn(g)

        final_v = St.v
        final_s = St.s

        @block.vector
        def _(v):
            vw = _SerEng(v, sem_v)
            for fn in prog["vector"]:
                fn(vw)
            assert vw._n == final_v, f"vector inc count {vw._n} != bookkeeping {final_v}"

        @block.scalar
        def _(s):
            sw = _SerEng(s, sem_s)
            for fn in prog["scalar"]:
                fn(sw)
            assert sw._n == final_s, f"scalar inc count {sw._n} != bookkeeping {final_s}"

        @block.tensor
        def _(t):
            for fn in prog["tensor"]:
                fn(t)

    nc.finalize()
    return nc


_NC_CACHE = None


def kernel(L0_attn_minibatch, L0_attn_Wq, L0_attn_bq, L0_attn_Wk, L0_attn_bk,
           L0_attn_Wv, L0_attn_bv, L0_ffn_weight, L0_ffn_bias):
    global _NC_CACHE
    x = np.ascontiguousarray(L0_attn_minibatch, dtype=np.float32)
    shards = [np.ascontiguousarray(x[2 * c:2 * c + 2]) for c in range(8)]
    common = {
        "wq": np.ascontiguousarray(L0_attn_Wq, dtype=np.float32),
        "wk": np.ascontiguousarray(L0_attn_Wk, dtype=np.float32),
        "wv": np.ascontiguousarray(L0_attn_Wv, dtype=np.float32),
        "wf": np.ascontiguousarray(L0_ffn_weight, dtype=np.float32),
        "bq": np.ascontiguousarray(L0_attn_bq, dtype=np.float32).reshape(1, D),
        "bk": np.ascontiguousarray(L0_attn_bk, dtype=np.float32).reshape(1, D),
        "bv": np.ascontiguousarray(L0_attn_bv, dtype=np.float32).reshape(1, D),
        "bf": np.ascontiguousarray(L0_ffn_bias, dtype=np.float32).reshape(1, D),
    }
    if _NC_CACHE is None:
        _NC_CACHE = build_program()
    nc = _NC_CACHE
    in_maps = [{"x": shards[c], "xf": x, **common} for c in range(8)]
    res = run_bass_kernel_spmd(nc, in_maps, core_ids=list(range(8)))
    return np.concatenate([res.results[c]["y"] for c in range(8)], axis=0)
